# version 01 (final)
# baseline (speedup 1.0000x reference)
import numpy as np
import ml_dtypes
import contextlib
import concourse.bass as bass
import concourse.mybir as mybir
from concourse.bass_utils import run_bass_kernel_spmd

F32 = mybir.dt.float32
BF16 = mybir.dt.bfloat16
AF = mybir.ActivationFunctionType
ALU = mybir.AluOpType
AX = mybir.AxisListType
BF = ml_dtypes.bfloat16

NCORE = 8
T = 2048
NSUB = 4
TS = T // NSUB
D = 1024
DC = D // 128
DI = 2048
DIC = DI // 128
DSTATE = 16
DTRANK = 64
I0 = 2816
I0C = I0 // 128
I1 = 3072
I1C = I1 // 128
NHEAD = 16
BLK = 64
EPS = 1e-5
MAGIC = float(3 << 22)

ENGS = ("sync", "tensor", "vector", "scalar", "gpsimd")


class Prog:

    def __init__(self, nc, sS, sD, sC):
        self.nc = nc
        self.sS, self.sD, self.sC = sS, sD, sC
        self.ops = {e: [] for e in ENGS}
        self.stage_total = 0
        self.dma_total = 0
        self.cc_total = 0
        self.cur = {e: None for e in ENGS}
        self.cur_dma = 0
        self.cur_cc = 0
        self.entered = set()
        self.nstages = 0

    def _enter(self, e):
        if e in self.entered:
            return
        self.entered.add(e)
        st, dt, ct = self.stage_total, 16 * self.dma_total, self.cc_total
        if st:
            self.ops[e].append([lambda eng, v=st: eng.wait_ge(self.sS, v), []])
        if dt:
            self.ops[e].append([lambda eng, v=dt: eng.wait_ge(self.sD, v), []])
        if ct:
            self.ops[e].append([lambda eng, v=ct: eng.wait_ge(self.sC, v), []])

    def ins(self, e, fn):
        self._enter(e)
        self.ops[e].append([fn, []])
        self.cur[e] = "ins"

    def dma(self, out, in_, q="sync", **kw):
        self._enter(q)
        self.ops[q].append([lambda eng, o=out, i=in_, kw=kw: eng.dma_start(out=o, in_=i, **kw),
                            [(self.sD, 16)]])
        self.cur_dma += 1
        self.cur[q] = "dma"

    def cc(self, kind, op, ins_, outs):
        self._enter("gpsimd")
        self.ops["gpsimd"].append([
            lambda eng, k=kind, o=op, i=ins_, u=outs: eng.collective_compute(
                k, o, [list(range(NCORE))], i, u), [(self.sC, 1)]])
        self.cur_cc += 1
        self.cur["gpsimd"] = "cc"

    def stage(self):
        n = 0
        for e in ENGS:
            last = self.cur[e]
            if last is None:
                continue
            if last == "ins":
                self.ops[e][-1][1].append((self.sS, 1))
            else:
                self.ops[e].append([lambda eng: eng.nop(), [(self.sS, 1)]])
            n += 1
        if n == 0:
            return
        self.stage_total += n
        self.dma_total += self.cur_dma
        self.cc_total += self.cur_cc
        self.cur = {e: None for e in ENGS}
        self.cur_dma = 0
        self.cur_cc = 0
        self.entered = set()
        self.nstages += 1

    def replay(self, block):
        def runner(ops):
            def run(eng):
                for fn, extras in ops:
                    instr = fn(eng)
                    for sem, val in extras:
                        instr.then_inc(sem, val)
            return run
        if self.ops["sync"]:
            block.sync(runner(self.ops["sync"]))
        if self.ops["tensor"]:
            block.tensor(runner(self.ops["tensor"]))
        if self.ops["vector"]:
            block.vector(runner(self.ops["vector"]))
        if self.ops["scalar"]:
            block.scalar(runner(self.ops["scalar"]))
        if self.ops["gpsimd"]:
            block.gpsimd(runner(self.ops["gpsimd"]))



def emit_norm(p, sb, x_of_c, out_of_c, with_mean=True, gamma_col_of_c=None,
              scr_x=None, amax_only=False):
    HBF = sb["hb"][:, :, :].rearrange("p a b -> p (a b)")
    if scr_x is None:
        scr_x = lambda c: HBF[:, 0:4096].rearrange("p (c t) -> p c t", t=TS)[:, c, :]
    lnq = HBF[:, 4096:8192].rearrange("p (c t) -> p c t", t=TS)
    psB = sb["psB"]
    for s in range(NSUB):
        sl = slice(s * TS, (s + 1) * TS)
        for c in range(DC):
            if with_mean:
                p.ins("vector", lambda e, c=c, sl=sl: e.tensor_copy(
                    out=scr_x(c), in_=x_of_c(c)[:, sl]))
            p.ins("scalar", lambda e, c=c, sl=sl: e.activation(
                out=lnq[:, c, :], in_=x_of_c(c)[:, sl], func=AF.Square))
        p.stage()
        for c in range(DC):
            if with_mean:
                p.ins("tensor", lambda e, c=c, sl=sl: e.matmul(
                    psB[0:1, 0:TS], sb["ones128_bf"][:, :],
                    scr_x(c), start=(c == 0), stop=(c == DC - 1)))
        for c in range(DC):
            p.ins("tensor", lambda e, c=c, sl=sl: e.matmul(
                psB[0:1, TS:2 * TS], sb["ones128_bf"][:, :],
                lnq[:, c, :], start=(c == 0), stop=(c == DC - 1)))
        p.stage()
        if with_mean:
            p.ins("vector", lambda e, sl=sl: e.tensor_scalar_mul(
                out=sb["rowarena"][0:1, 0, sl], in0=psB[0:1, 0:TS], scalar1=1.0 / D))
        p.ins("vector", lambda e, sl=sl: e.tensor_scalar_mul(
            out=sb["rowarena"][0:1, 1, sl], in0=psB[0:1, TS:2 * TS], scalar1=1.0 / D))
        p.stage()
    if with_mean:
        p.ins("vector", lambda e: e.tensor_mul(
            out=sb["rowarena"][0:1, 2, :], in0=sb["rowarena"][0:1, 0, :], in1=sb["rowarena"][0:1, 0, :]))
        p.stage()
        p.ins("vector", lambda e: e.tensor_sub(
            out=sb["rowarena"][0:1, 1, :], in0=sb["rowarena"][0:1, 1, :], in1=sb["rowarena"][0:1, 2, :]))
        p.stage()
    p.ins("scalar", lambda e: e.activation(
        out=sb["rowarena"][0:1, 1, :], in_=sb["rowarena"][0:1, 1, :], func=AF.Ln, bias=EPS))
    p.stage()
    p.ins("scalar", lambda e: e.activation(
        out=sb["rowarena"][0:1, 2, :], in_=sb["rowarena"][0:1, 1, :], func=AF.Exp, scale=-0.5))
    p.stage()
    psC = sb["psC"]
    for s in range(NSUB):
        sl = slice(s * TS, (s + 1) * TS)
        if with_mean:
            p.ins("tensor", lambda e, sl=sl: e.matmul(
                psC[:, 0:TS], sb["ones1_f"][:, :], sb["rowarena"][0:1, 0, sl],
                start=True, stop=True))
        p.ins("tensor", lambda e, sl=sl: e.matmul(
            psB[:, 0:TS], sb["ones1_f"][:, :], sb["rowarena"][0:1, 2, sl],
            start=True, stop=True))
        p.stage()
        tmp = sb["tmpT"][:, 1536:2048]
        if amax_only:
            for c in range(DC):
                p.ins("vector", lambda e, c=c, sl=sl: e.tensor_sub(
                    out=tmp[:, :], in0=x_of_c(c)[:, sl], in1=psC[:, 0:TS]))
                p.ins("vector", lambda e: e.tensor_mul(
                    out=tmp[:, :], in0=tmp[:, :], in1=psB[:, 0:TS]))
                if s == 0 and c == 0:
                    p.ins("vector", lambda e: e.reduce_max(
                        out=sb["colA"][:, :], in_=tmp[:, :], axis=AX.X,
                        apply_absolute_value=True))
                else:
                    p.ins("vector", lambda e: e.tensor_reduce(
                        out=sb["colB"][:, :], in_=tmp[:, :], axis=AX.X,
                        op=ALU.max, apply_absolute_value=True))
                    p.ins("vector", lambda e: e.tensor_max(
                        out=sb["colA"][:, :], in0=sb["colA"][:, :],
                        in1=sb["colB"][:, :]))
            p.stage()
            continue
        for c in range(DC):
            if with_mean:
                p.ins("vector", lambda e, c=c, sl=sl: e.tensor_sub(
                    out=tmp[:, :], in0=x_of_c(c)[:, sl], in1=psC[:, 0:TS]))
                if gamma_col_of_c is None:
                    p.ins("vector", lambda e, c=c, sl=sl: e.tensor_mul(
                        out=out_of_c(c)[:, sl], in0=tmp[:, :], in1=psB[:, 0:TS]))
                else:
                    p.ins("vector", lambda e, c=c, sl=sl: e.scalar_tensor_tensor(
                        out=out_of_c(c)[:, sl], in0=tmp[:, :],
                        scalar=gamma_col_of_c(c), in1=psB[:, 0:TS],
                        op0=ALU.mult, op1=ALU.mult))
            else:
                if gamma_col_of_c is None:
                    p.ins("vector", lambda e, c=c, sl=sl: e.tensor_mul(
                        out=out_of_c(c)[:, sl], in0=x_of_c(c)[:, sl], in1=psB[:, 0:TS]))
                else:
                    p.ins("vector", lambda e, c=c, sl=sl: e.scalar_tensor_tensor(
                        out=out_of_c(c)[:, sl], in0=x_of_c(c)[:, sl],
                        scalar=gamma_col_of_c(c), in1=psB[:, 0:TS],
                        op0=ALU.mult, op1=ALU.mult))
        p.stage()


def emit_colmax_to_scalar(p, sb, dst):
    psC = sb["psC"]
    p.ins("tensor", lambda e: e.transpose(
        psC[0:1, 0:128], sb["colA"][:, :], sb["identf"][:, :]))
    p.stage()
    p.ins("vector", lambda e: e.reduce_max(
        out=sb[dst][0:1, 0:1], in_=psC[0:1, 0:128], axis=AX.X))
    p.stage()


def emit_allreduce_max_scalar(p, sb, dr, src):
    p.dma(out=dr["c_in1"][0:1, 0:1], in_=sb[src][0:1, 0:1])
    p.stage()
    p.cc("AllReduce", ALU.max, [dr["c_in1"].opt()], [dr["c_out1"].opt()])
    p.stage()
    p.dma(out=sb[src][0:1, 0:1], in_=dr["c_out1"][0:1, 0:1])
    p.stage()


def emit_bcast_scale_cols(p, sb, n):
    psC = sb["psC"]
    p.ins("tensor", lambda e: e.matmul(
        psC[:, 0:16], sb["ones1_f"][:, :], sb["rowS"][0:1, 0:16],
        start=True, stop=True))
    p.stage()
    p.ins("vector", lambda e: e.tensor_copy(
        out=sb["colS"][:, 0:16], in_=psC[:, 0:16]))
    p.stage()


def emit_quant3(p, eng, out_ap, in_ap, rs_col, s_col, lo, hi, tmp):
    p.ins(eng, lambda e: e.tensor_scalar(
        out=tmp, in0=in_ap, scalar1=rs_col, scalar2=MAGIC,
        op0=ALU.mult, op1=ALU.add))
    p.ins(eng, lambda e: e.tensor_scalar(
        out=tmp, in0=tmp, scalar1=MAGIC, scalar2=float(lo),
        op0=ALU.subtract, op1=ALU.max))
    p.ins(eng, lambda e: e.tensor_scalar(
        out=out_ap, in0=tmp, scalar1=float(hi), scalar2=s_col,
        op0=ALU.min, op1=ALU.mult))


def emit_quant2(p, eng, out_ap, in_ap, rs_col, s_col, tmp):
    p.ins(eng, lambda e: e.tensor_scalar(
        out=tmp, in0=in_ap, scalar1=rs_col, scalar2=MAGIC,
        op0=ALU.mult, op1=ALU.add))
    p.ins(eng, lambda e: e.tensor_scalar(
        out=out_ap, in0=tmp, scalar1=MAGIC, scalar2=s_col,
        op0=ALU.subtract, op1=ALU.mult))


def emit_weight_prep(p, sb, w_nat, n_rows, n_cols, qmax, dst, blk=None,
                     scale_dst=None):
    RC = n_rows // 128
    CC = n_cols // 128
    for rc in range(RC):
        p.dma(out=sb["wprep"][:, 0:n_cols], in_=w_nat[rc * 128:(rc + 1) * 128, :])
        p.stage()
        wrow = sb["wprep"][:, 0:n_cols]
        wq = sb["hq_f"][:, :, :].rearrange("p a b -> p (a b)")[:, 0:n_cols]
        if blk is None:
            p.ins("vector", lambda e: e.reduce_max(
                out=sb["colA"][:, :], in_=wrow, axis=AX.X,
                apply_absolute_value=True))
            p.stage()
            p.ins("vector", lambda e: e.tensor_scalar_mul(
                out=sb["colB"][:, :], in0=sb["colA"][:, :], scalar1=1.0 / qmax))
            p.ins("vector", lambda e: e.reciprocal(
                out=sb["colC"][:, :], in_=sb["colA"][:, :]))
            p.stage()
            p.ins("vector", lambda e: e.tensor_scalar_mul(
                out=sb["colC"][:, :], in0=sb["colC"][:, :], scalar1=float(qmax)))
            p.stage()
            if scale_dst is not None:
                p.ins("vector", lambda e, rc=rc: e.tensor_copy(
                    out=sb[scale_dst][:, rc:rc + 1], in_=sb["colB"][:, :]))
            for c0 in range(0, n_cols, 1024):
                c1 = min(c0 + 1024, n_cols)
                p.ins("vector", lambda e, c0=c0, c1=c1: e.tensor_scalar(
                    out=sb["tmpT"][:, 0:c1 - c0], in0=wrow[:, c0:c1],
                    scalar1=sb["colC"][:, :], scalar2=MAGIC,
                    op0=ALU.mult, op1=ALU.add))
                if scale_dst is None:
                    p.ins("vector", lambda e, c0=c0, c1=c1: e.tensor_scalar(
                        out=wq[:, c0:c1], in0=sb["tmpT"][:, 0:c1 - c0],
                        scalar1=MAGIC, scalar2=sb["colB"][:, :],
                        op0=ALU.subtract, op1=ALU.mult))
                else:
                    p.ins("vector", lambda e, c0=c0, c1=c1: e.tensor_scalar_sub(
                        out=wq[:, c0:c1], in0=sb["tmpT"][:, 0:c1 - c0],
                        scalar1=MAGIC))
            p.stage()
        else:
            G = n_cols // blk
            wg = wrow.rearrange("p (g w) -> p g w", w=blk)
            p.ins("vector", lambda e: e.tensor_reduce(
                out=sb["rowarena"][:, 2, 0:G], in_=wg, axis=AX.X, op=ALU.max,
                apply_absolute_value=True))
            p.stage()
            p.ins("vector", lambda e: e.tensor_scalar(
                out=sb["rowarena"][:, 2, 512:512 + G], in0=sb["rowarena"][:, 2, 0:G],
                scalar1=1.0 / qmax, scalar2=1e-12, op0=ALU.mult, op1=ALU.max))
            p.stage()
            p.ins("vector", lambda e: e.reciprocal(
                out=sb["rowarena"][:, 2, 1024:1024 + G], in_=sb["rowarena"][:, 2, 512:512 + G]))
            p.stage()
            for c0 in range(0, n_cols, 2048):
                c1 = min(c0 + 2048, n_cols)
                g0, g1 = c0 // blk, c1 // blk
                gsb = sb["rowarena"][:, 2, 512 + g0:512 + g1].rearrange("p g -> p g ()").broadcast_to([128, g1 - g0, blk])
                grsb = sb["rowarena"][:, 2, 1024 + g0:1024 + g1].rearrange("p g -> p g ()").broadcast_to([128, g1 - g0, blk])
                tw3 = sb["tmpT"][:, 0:c1 - c0].rearrange("p (g w) -> p g w", w=blk)
                wg3 = wq[:, c0:c1].rearrange("p (g w) -> p g w", w=blk)
                wgs = wrow[:, c0:c1].rearrange("p (g w) -> p g w", w=blk)
                p.ins("vector", lambda e, tw3=tw3, wgs=wgs, grsb=grsb: e.tensor_tensor(
                    out=tw3, in0=wgs, in1=grsb, op=ALU.mult))
                p.stage()
                p.ins("vector", lambda e, c0=c0, c1=c1: e.tensor_scalar(
                    out=sb["tmpT"][:, 0:c1 - c0], in0=sb["tmpT"][:, 0:c1 - c0],
                    scalar1=MAGIC, scalar2=MAGIC, op0=ALU.add, op1=ALU.subtract))
                p.stage()
                p.ins("vector", lambda e, wg3=wg3, tw3=tw3, gsb=gsb: e.tensor_tensor(
                    out=wg3, in0=tw3, in1=gsb, op=ALU.mult))
                p.stage()
        psT = sb["psC"]
        for cc0 in range(0, CC, 4):
            cce = min(cc0 + 4, CC)
            p.ins("vector", lambda e, cc0=cc0, cce=cce: e.tensor_copy(
                out=sb["wprep"][:, 0:(cce - cc0) * 128],
                in_=wq[:, cc0 * 128:cce * 128]))
            p.stage()
            for g2 in range(cc0, cce, 2):
                g2e = min(g2 + 2, cce)
                for cc in range(g2, g2e):
                    p.ins("tensor", lambda e, cc=cc, cc0=cc0, g2=g2: e.transpose(
                        psT[:, (cc - g2) * 128:(cc - g2 + 1) * 128],
                        sb["wprep"][:, (cc - cc0) * 128:(cc - cc0 + 1) * 128],
                        sb["identf"][:, :]))
                p.stage()
                for cc in range(g2, g2e):
                    p.ins("vector", lambda e, cc=cc, g2=g2: e.tensor_copy(
                        out=sb["wtev"][:, (cc - g2) * 128:(cc - g2 + 1) * 128],
                        in_=psT[:, (cc - g2) * 128:(cc - g2 + 1) * 128]))
                p.stage()
                for cc in range(g2, g2e):
                    p.dma(out=dst[cc * 128:(cc + 1) * 128, rc * 128:(rc + 1) * 128],
                          in_=sb["wtev"][:, (cc - g2) * 128:(cc - g2 + 1) * 128])
            p.stage()


def emit_ffn0(p, sb, dr, first_call):
    nc = p.nc
    psA, psB, psC = sb["psA"], sb["psB"], sb["psC"]
    x_of_c = lambda c: sb["x"][:, c, :]
    if first_call:
        emit_weight_prep(p, sb, dr["f0g"], I0, D, 127.0, dr["wc0g"], scale_dst="swg")
        emit_weight_prep(p, sb, dr["f0u"], I0, D, 127.0, dr["wc0u"], scale_dst="swu")
        emit_weight_prep(p, sb, dr["f0d"], D, I0, 127.0, dr["wc0d"], scale_dst="swd")
    emit_norm(p, sb, x_of_c, None, with_mean=True, amax_only=True)
    emit_colmax_to_scalar(p, sb, "sc0")
    emit_allreduce_max_scalar(p, sb, dr, "sc0")
    p.ins("scalar", lambda e: e.activation(
        out=sb["rowS"][0:1, 2:3], in_=sb["clipx"][0:1, 0:1], func=AF.Sigmoid))
    p.stage()
    p.ins("vector", lambda e: e.tensor_mul(
        out=sb["rowS"][0:1, 0:1], in0=sb["sc0"][0:1, 0:1], in1=sb["rowS"][0:1, 2:3]))
    p.stage()
    p.ins("vector", lambda e: e.tensor_scalar_mul(
        out=sb["rowS"][0:1, 0:1], in0=sb["rowS"][0:1, 0:1], scalar1=1.0 / 127.0))
    p.stage()
    p.ins("vector", lambda e: e.reciprocal(
        out=sb["rowS"][0:1, 1:2], in_=sb["rowS"][0:1, 0:1]))
    p.stage()
    emit_bcast_scale_cols(p, sb, 2)
    s_col, rs_col = sb["colS"][:, 0:1], sb["colS"][:, 1:2]
    p.ins("vector", lambda e: e.tensor_scalar_mul(
        out=sb["rowarena"][0:1, 1, :], in0=sb["rowarena"][0:1, 2, :],
        scalar1=sb["rowS"][0:1, 1:2]))
    p.stage()
    p.ins("vector", lambda e: e.tensor_mul(
        out=sb["rowarena"][0:1, 0, :], in0=sb["rowarena"][0:1, 1, :], in1=sb["rowarena"][0:1, 0, :]))
    p.stage()
    p.ins("vector", lambda e: e.tensor_scalar_mul(
        out=sb["rowarena"][0:1, 0, :], in0=sb["rowarena"][0:1, 0, :], scalar1=-1.0))
    p.stage()
    psB, psC2 = sb["psB"], sb["psC"]
    for s in range(NSUB):
        sl = slice(s * TS, (s + 1) * TS)
        p.ins("tensor", lambda e, sl=sl: e.matmul(
            psB[:, 0:TS], sb["ones1_f"][:, :], sb["rowarena"][0:1, 1, sl],
            start=True, stop=True))
        p.ins("tensor", lambda e, sl=sl: e.matmul(
            psC2[:, 0:TS], sb["ones1_f"][:, :], sb["rowarena"][0:1, 0, sl],
            start=True, stop=True))
        p.stage()
        for c in range(DC):
            p.ins("vector", lambda e, c=c, sl=sl: e.tensor_mul(
                out=sb["tmpT"][:, 1536:2048][:, :], in0=x_of_c(c)[:, sl], in1=psB[:, 0:TS]))
            p.ins("vector", lambda e: e.tensor_add(
                out=sb["tmpT"][:, 1536:2048][:, :], in0=sb["tmpT"][:, 1536:2048][:, :], in1=psC2[:, 0:TS]))
            p.ins("vector", lambda e: e.tensor_scalar(
                out=sb["tmpT"][:, 1536:2048][:, :], in0=sb["tmpT"][:, 1536:2048][:, :], scalar1=MAGIC,
                scalar2=MAGIC, op0=ALU.add, op1=ALU.subtract))
            p.ins("vector", lambda e, c=c, sl=sl: e.tensor_scalar(
                out=sb["hb"][:, c, sl], in0=sb["tmpT"][:, 1536:2048][:, :], scalar1=-128.0,
                scalar2=127.0, op0=ALU.max, op1=ALU.min))
        p.stage()
    p.ins("vector", lambda e: e.tensor_scalar(
        out=sb["sall"][:, 0:I0C], in0=sb["swg"][:, 0:I0C], scalar1=s_col,
        scalar2=None, op0=ALU.mult))
    p.ins("vector", lambda e: e.tensor_scalar(
        out=sb["sall"][:, 24:24 + I0C], in0=sb["swu"][:, 0:I0C], scalar1=s_col,
        scalar2=None, op0=ALU.mult))
    p.stage()
    first_max = True
    for ic in range(I0C):
        for k in range(DC):
            p.dma(out=sb["wA"][:, k, :],
                  in_=dr["wc0g"][k * 128:(k + 1) * 128, ic * 128:(ic + 1) * 128])
            p.dma(out=sb["wA"][:, DC + k, :],
                  in_=dr["wc0u"][k * 128:(k + 1) * 128, ic * 128:(ic + 1) * 128])
        p.stage()
        for half in range(2):
            hsl = slice(half * (T // 2), (half + 1) * (T // 2))
            for s2 in range(2):
                sl = slice(half * (T // 2) + s2 * TS, half * (T // 2) + (s2 + 1) * TS)
                po = slice(s2 * TS, (s2 + 1) * TS)
                pu = slice(1024 + s2 * TS, 1024 + (s2 + 1) * TS)
                for k in range(DC):
                    p.ins("tensor", lambda e, k=k, sl=sl, po=po: e.matmul(
                        psA[:, po], sb["wA"][:, k, :], sb["hb"][:, k, sl],
                        start=(k == 0), stop=(k == DC - 1)))
                for k in range(DC):
                    p.ins("tensor", lambda e, k=k, sl=sl, pu=pu: e.matmul(
                        psA[:, pu], sb["wA"][:, DC + k, :], sb["hb"][:, k, sl],
                        start=(k == 0), stop=(k == DC - 1)))
            p.stage()
            p.ins("scalar", lambda e, ic=ic: e.activation(
                out=sb["tmpT"][:, 0:1024], in_=psA[:, 0:1024], func=AF.Silu,
                scale=sb["sall"][:, ic:ic + 1]))
            p.stage()
            p.ins("vector", lambda e, ic=ic: e.scalar_tensor_tensor(
                out=sb["tband"][:, 0:1024], in0=psA[:, 1024:2048],
                scalar=sb["sall"][:, 24 + ic:24 + ic + 1], in1=sb["tmpT"][:, 0:1024],
                op0=ALU.mult, op1=ALU.mult))
            p.stage()
            p.dma(out=dr["big0"][ic * 128:(ic + 1) * 128, hsl],
                  in_=sb["tband"][:, 0:1024])
            if first_max:
                p.ins("vector", lambda e: e.reduce_max(
                    out=sb["colA"][:, :], in_=sb["tband"][:, 0:1024], axis=AX.X,
                    apply_absolute_value=True))
                first_max = False
            else:
                p.ins("vector", lambda e: e.tensor_reduce(
                    out=sb["colB"][:, :], in_=sb["tband"][:, 0:1024], axis=AX.X,
                    op=ALU.max, apply_absolute_value=True))
                p.ins("vector", lambda e: e.tensor_max(
                    out=sb["colA"][:, :], in0=sb["colA"][:, :], in1=sb["colB"][:, :]))
            p.stage()
    emit_colmax_to_scalar(p, sb, "sc0")
    emit_allreduce_max_scalar(p, sb, dr, "sc0")
    p.ins("scalar", lambda e: e.activation(
        out=sb["rowS"][0:1, 2:3], in_=sb["clipd"][0:1, 0:1], func=AF.Sigmoid))
    p.stage()
    p.ins("vector", lambda e: e.tensor_mul(
        out=sb["rowS"][0:1, 0:1], in0=sb["sc0"][0:1, 0:1], in1=sb["rowS"][0:1, 2:3]))
    p.stage()
    p.ins("vector", lambda e: e.tensor_scalar_mul(
        out=sb["rowS"][0:1, 0:1], in0=sb["rowS"][0:1, 0:1], scalar1=1.0 / 127.0))
    p.stage()
    p.ins("vector", lambda e: e.reciprocal(
        out=sb["rowS"][0:1, 1:2], in_=sb["rowS"][0:1, 0:1]))
    p.stage()
    emit_bcast_scale_cols(p, sb, 2)
    p.ins("vector", lambda e: e.tensor_scalar(
        out=sb["sall"][:, 48:48 + DC], in0=sb["swd"][:, 0:DC], scalar1=s_col,
        scalar2=0.5, op0=ALU.mult, op1=ALU.mult))
    p.stage()
    for s in range(NSUB):
        sl = slice(s * TS, (s + 1) * TS)
        p.dma(out=sb["hq_f"][:, 0:I0C, :],
              in_=dr["big0"].rearrange("(c p) t -> p c t", p=128)[:, 0:I0C, sl])
        p.stage()
        for ic in range(I0C):
            p.ins("vector", lambda e, ic=ic: e.tensor_scalar(
                out=sb["tmpT"][:, 1536:2048][:, :], in0=sb["hq_f"][:, ic, :], scalar1=rs_col,
                scalar2=MAGIC, op0=ALU.mult, op1=ALU.add))
            p.ins("vector", lambda e: e.tensor_scalar(
                out=sb["tmpT"][:, 1536:2048][:, :], in0=sb["tmpT"][:, 1536:2048][:, :], scalar1=MAGIC,
                scalar2=-128.0, op0=ALU.subtract, op1=ALU.max))
            p.ins("vector", lambda e, ic=ic: e.tensor_scalar_min(
                out=sb["hq_f"][:, ic, :], in0=sb["tmpT"][:, 1536:2048][:, :], scalar1=127.0))
        p.stage()
        for c in range(DC):
            p.dma(out=sb["wdown"][:, 0:I0C, :],
                  in_=dr["wc0d"].rearrange("(ic p) m -> p ic m", p=128)[:, 0:I0C, c * 128:(c + 1) * 128])
            p.stage()
            for ic in range(I0C):
                p.ins("tensor", lambda e, c=c, ic=ic: e.matmul(
                    psC[:, 0:TS], sb["wdown"][:, ic, :],
                    sb["hq_f"][:, ic, :], start=(ic == 0), stop=(ic == I0C - 1)))
            p.stage()
            p.ins("vector", lambda e, c=c, sl=sl: e.scalar_tensor_tensor(
                out=sb["x"][:, c, sl], in0=psC[:, 0:TS], scalar=sb["sall"][:, 48 + c:48 + c + 1],
                in1=sb["x"][:, c, sl], op0=ALU.mult, op1=ALU.add))
            p.stage()




def emit_cast_weight(p, sb, dr, src_ap, dst_ap, n_rows, n_cols):
    for rc in range(n_rows // 128):
        for c0 in range(0, n_cols, 2048):
            c1 = min(c0 + 2048, n_cols)
            p.dma(out=sb["tmpT"][:, 0:c1 - c0],
                  in_=src_ap[rc * 128:(rc + 1) * 128, c0:c1])
            p.stage()
            p.ins("gpsimd", lambda e, c0=c0, c1=c1: e.tensor_copy(
                out=sb["tband"][:, 0:c1 - c0], in_=sb["tmpT"][:, 0:c1 - c0]))
            p.stage()
            p.dma(out=dst_ap[rc * 128:(rc + 1) * 128, c0:c1],
                  in_=sb["tband"][:, 0:c1 - c0])
            p.stage()


def emit_mamba(p, sb, dr):
    x_of_c = lambda c: sb["x"][:, c, :]
    for c in range(DC):
        p.dma(out=dr["d_x"][c * 128:(c + 1) * 128, :], in_=x_of_c(c))
    p.stage()
    emit_norm(p, sb, x_of_c, x_of_c, with_mean=True)
    for li in range(2):
        emit_mamba_layer(p, sb, dr, li)
    for c in range(DC):
        p.dma(out=sb["tmpT"][:, :], in_=dr["d_x"][c * 128:(c + 1) * 128, :])
        p.stage()
        p.ins("vector", lambda e, c=c: e.tensor_add(
            out=sb["x"][:, c, :], in0=sb["x"][:, c, :], in1=sb["tmpT"][:, :]))
        p.stage()


def emit_mamba_layer(p, sb, dr, li):
    nc = p.nc
    psA, psB, psC = sb["psA"], sb["psB"], sb["psC"]
    a_of_c = lambda c: sb["x"][:, c, :]
    WP = sb["wprep"]
    ACOL = WP[:, 0:256]
    SDEL = WP[:, 256:272]
    SDT = WP[:, 272:288]
    PPROD = WP[:, 320:576]
    HLOC = WP[:, 576:832]
    HIN = WP[:, 832:1088]
    CARRY = WP[:, 1088:1104]
    RTMP = WP[:, 1104:1360]
    RTMP2 = WP[:, 1360:1616]
    QB = [sb["rowarena"][:, 1, 0:512], sb["rowarena"][:, 1, 512:1024]]

    emit_cast_weight(p, sb, dr, dr["l0inT"][li], dr["d_win"], D, 2 * DI)
    emit_cast_weight(p, sb, dr, dr["l0outT"][li], dr["d_wout"], DI, D)
    p.dma(out=sb["rowarena"][:, 2, 0:256], in_=dr["l0alog"][:, li * 256:(li + 1) * 256])
    p.stage()
    p.ins("scalar", lambda e: e.activation(
        out=sb["tmpT"][:, 0:256], in_=sb["rowarena"][:, 2, 0:256], func=AF.Exp))
    p.stage()
    p.ins("vector", lambda e: e.tensor_scalar_mul(
        out=ACOL, in0=sb["tmpT"][:, 0:256], scalar1=-1.0))
    p.ins("vector", lambda e: e.memset(SDEL, 0.0))
    p.stage()
    p.dma(out=sb["tmpT"][:, 0:1536].rearrange("p (k m) -> p k m", m=96),
          in_=dr["l0xpT"][li].rearrange("(k p) m -> p k m", p=128))
    p.stage()
    p.ins("vector", lambda e: e.tensor_copy(
        out=sb["wdown"][:, :, :].rearrange("p a b -> p (a b)")[:, 2560:4096].rearrange("p (k m) -> p k m", m=96)[:, :, :],
        in_=sb["tmpT"][:, 0:1536].rearrange("p (k m) -> p k m", m=96)))
    p.stage()
    p.dma(out=sb["tmpT"][0:64, :], in_=dr["l0dtT"][li])
    p.stage()
    p.ins("vector", lambda e: e.tensor_copy(
        out=sb["wdown"][0:64, 0:16, :],
        in_=sb["tmpT"][0:64, :].rearrange("k (mc m) -> k mc m", m=128)))
    p.stage()

    WIN = dr["d_win"].rearrange("(kc p) mm -> p kc mm", p=128)
    for s in range(NSUB):
        sl = slice(s * TS, (s + 1) * TS)
        for c in range(DC):
            p.ins("scalar", lambda e, c=c, sl=sl: e.activation(
                out=sb["hb"][:, :, :].rearrange("p a b -> p (a b)")[:, 4096:8192].rearrange("p (c t) -> p c t", t=TS)[:, c, :], in_=a_of_c(c)[:, sl], func=AF.Square))
        p.stage()
        for c in range(DC):
            p.ins("tensor", lambda e, c=c: e.matmul(
                psB[0:1, 0:TS], sb["ones128_bf"][:, :], sb["hb"][:, :, :].rearrange("p a b -> p (a b)")[:, 4096:8192].rearrange("p (c t) -> p c t", t=TS)[:, c, :],
                start=(c == 0), stop=(c == DC - 1)))
        p.stage()
        p.ins("vector", lambda e: e.tensor_scalar_mul(
            out=sb["rowarena"][0:1, 1, 0:TS], in0=psB[0:1, 0:TS], scalar1=1.0 / D))
        p.stage()
        p.ins("scalar", lambda e: e.activation(
            out=sb["rowarena"][0:1, 1, 0:TS], in_=sb["rowarena"][0:1, 1, 0:TS],
            func=AF.Ln, bias=EPS))
        p.stage()
        p.ins("scalar", lambda e: e.activation(
            out=sb["rowarena"][0:1, 2, 0:TS], in_=sb["rowarena"][0:1, 1, 0:TS],
            func=AF.Exp, scale=-0.5))
        p.stage()
        p.ins("tensor", lambda e: e.matmul(
            psB[:, 0:TS], sb["ones1_f"][:, :], sb["rowarena"][0:1, 2, 0:TS],
            start=True, stop=True))
        p.stage()
        for c in range(DC):
            p.ins("vector", lambda e, c=c, sl=sl: e.scalar_tensor_tensor(
                out=sb["hb"][:, :, :].rearrange("p a b -> p (a b)")[:, 0:4096].rearrange("p (c t) -> p c t", t=TS)[:, c, :], in0=a_of_c(c)[:, sl],
                scalar=sb["nwcol"][:, li, c:c + 1], in1=psB[:, 0:TS],
                op0=ALU.mult, op1=ALU.mult))
        p.stage()
        for m in range(33):
            if m < 32:
                buf = m % 2
                if m == 0:
                    p.dma(out=sb["wA"][:, 0:8, :], in_=WIN[:, :, 0:128])
                    p.stage()
                if m + 1 < 32:
                    p.dma(out=sb["wA"][:, 8 * (1 - buf):8 * (1 - buf) + 8, :],
                          in_=WIN[:, :, (m + 1) * 128:(m + 2) * 128])
                bank = m % 4
                for k in range(DC):
                    p.ins("tensor", lambda e, k=k, buf=buf, bank=bank: e.matmul(
                        psA[:, bank * 512:bank * 512 + 512],
                        sb["wA"][:, 8 * buf + k, :], sb["hb"][:, :, :].rearrange("p a b -> p (a b)")[:, 0:4096].rearrange("p (c t) -> p c t", t=TS)[:, k, :],
                        start=(k == 0), stop=(k == DC - 1)))
            if m >= 1:
                mm = m - 1
                bank = mm % 4
                if mm < 16:
                    p.ins("vector", lambda e, mm=mm, bank=bank: e.tensor_copy(
                        out=sb["hq_f"][:, mm, 0:TS],
                        in_=psA[:, bank * 512:bank * 512 + 512]))
                else:
                    p.ins("scalar", lambda e, mm=mm, bank=bank: e.activation(
                        out=sb["hq_f"][:, :, :].rearrange("p a b -> p (a b)")[:, 8192:16384].rearrange("p (c t) -> p c t", t=TS)[:, mm - 16, :],
                        in_=psA[:, bank * 512:bank * 512 + 512], func=AF.Silu))
            p.stage()
        p.dma(out=dr["d_pre"].rearrange("(c p) t -> p c t", p=128)[:, :, sl],
              in_=sb["hq_f"][:, 0:16, 0:TS])
        p.dma(out=dr["d_zz"].rearrange("(c p) t -> p c t", p=128)[:, :, sl],
              in_=sb["hq_f"][:, :, :].rearrange("p a b -> p (a b)")[:, 8192:16384].rearrange("p (c t) -> p c t", t=TS)[:, :, :])
        p.stage()
        if s == NSUB - 1:
            p.ins("vector", lambda e: e.tensor_copy(
                out=sb["wprep"][:, 1700:1748][:, :].rearrange("p (c k) -> p c k", k=3),
                in_=sb["hq_f"][:, 0:16, TS - 3:TS]))
            p.stage()
            p.dma(out=dr["c_inh"], in_=sb["wprep"][:, 1700:1748][:, :])
            p.stage()
            p.cc("AllGather", ALU.bypass, [dr["c_inh"].opt()], [dr["c_outh"].opt()])
            p.stage()

    XPB = sb["hq_f"][:, :, :].rearrange("p a b -> p (a b)")
    xpre3 = XPB[:, 0:16 * 515].rearrange("p (c t) -> p c t", t=515)
    DSUB = sb["hb"][:, :, :].rearrange("p a b -> p (a b)")[:, 4096:12288].rearrange(
        "p (c t) -> p c t", t=TS)
    USUB = XPB[:, 8240:16432].rearrange("p (c t) -> p c t", t=TS)
    PRE = dr["d_pre"].rearrange("(c p) t -> p c t", p=128)
    for s in range(NSUB):
        sl = slice(s * TS, (s + 1) * TS)
        if s == 0:
            p.dma(out=xpre3[:, :, 3:515], in_=PRE[:, :, 0:TS])
            p.stage()
            p.ins("vector", lambda e: e.memset(sb["rowarena"][:, 2, 512:560][:, :], 0.0))
            p.stage()
            for k in range(NCORE):
                p.dma(out=sb["wprep"][:, 1700:1748][:, :], in_=dr["c_outh"][k * 128:(k + 1) * 128, :])
                p.stage()
                p.ins("vector", lambda e, k=k: e.scalar_tensor_tensor(
                    out=sb["rowarena"][:, 2, 512:560][:, :], in0=sb["wprep"][:, 1700:1748][:, :],
                    scalar=sb["cmask"][:, k:k + 1], in1=sb["rowarena"][:, 2, 512:560][:, :],
                    op0=ALU.mult, op1=ALU.add))
                p.stage()
            p.ins("vector", lambda e: e.tensor_copy(
                out=xpre3[:, :, 0:3],
                in_=sb["rowarena"][:, 2, 512:560][:, :].rearrange("p (c k) -> p c k", k=3)))
            p.stage()
        else:
            p.dma(out=xpre3[:, :, 0:515], in_=PRE[:, :, s * TS - 3:(s + 1) * TS])
            p.stage()
        for c0 in range(0, 16, 4):
            for c in range(c0, c0 + 4):
                qq = (c - c0) * 512
                p.ins("vector", lambda e, c=c, qq=qq: e.tensor_scalar_mul(
                    out=sb["tmpT"][:, qq:qq + 512], in0=xpre3[:, c, 0:512],
                    scalar1=sb["l0c"][:, li, c, 0:1]))
                for k in range(1, 4):
                    p.ins("vector", lambda e, c=c, k=k, qq=qq: e.scalar_tensor_tensor(
                        out=sb["tmpT"][:, qq:qq + 512], in0=xpre3[:, c, k:512 + k],
                        scalar=sb["l0c"][:, li, c, k:k + 1],
                        in1=sb["tmpT"][:, qq:qq + 512],
                        op0=ALU.mult, op1=ALU.add))
            p.stage()
            for c in range(c0, c0 + 4):
                qq = (c - c0) * 512
                p.ins("scalar", lambda e, c=c, qq=qq: e.activation(
                    out=xpre3[:, c, 3:515], in_=sb["tmpT"][:, qq:qq + 512],
                    func=AF.Silu, bias=sb["l0c"][:, li, c, 4:5]))
            p.stage()
        p.dma(out=dr["d_xc"].rearrange("(c p) t -> p c t", p=128)[:, :, sl],
              in_=xpre3[:, :, 3:515])
        for k in range(16):
            p.ins("tensor", lambda e, k=k: e.matmul(
                psB[0:64, 0:TS], sb["wdown"][:, :, :].rearrange("p a b -> p (a b)")[:, 2560:4096].rearrange("p (k m) -> p k m", m=96)[:, k, 0:64], xpre3[:, k, 3:515],
                start=(k == 0), stop=(k == 15)))
        for k in range(16):
            p.ins("tensor", lambda e, k=k: e.matmul(
                psB[0:16, 512:512 + TS], sb["wdown"][:, :, :].rearrange("p a b -> p (a b)")[:, 2560:4096].rearrange("p (k m) -> p k m", m=96)[:, k, 64:80], xpre3[:, k, 3:515],
                start=(k == 0), stop=(k == 15)))
        for k in range(16):
            p.ins("tensor", lambda e, k=k: e.matmul(
                psC[0:16, 0:TS], sb["wdown"][:, :, :].rearrange("p a b -> p (a b)")[:, 2560:4096].rearrange("p (k m) -> p k m", m=96)[:, k, 80:96], xpre3[:, k, 3:515],
                start=(k == 0), stop=(k == 15)))
        p.stage()
        p.ins("vector", lambda e: e.tensor_copy(out=sb["wdown"][0:64, 16:20, :].rearrange("p a b -> p (a b)"), in_=psB[0:64, 0:TS]))
        p.ins("scalar", lambda e: e.activation(
            out=sb["brow16"][:, 0:TS], in_=psB[0:16, 512:512 + TS], func=AF.Copy))
        p.stage()
        p.ins("scalar", lambda e: e.activation(
            out=sb["brow16"][:, TS:2 * TS], in_=psC[0:16, 0:TS], func=AF.Copy))
        p.stage()
        BT2 = sb["hb"][:, :, :].rearrange("p a b -> p (a b)")
        p.ins("tensor", lambda e: e.matmul(
            psA[:, 0:TS], sb["selB"][:, :], sb["brow16"][:, 0:TS],
            start=True, stop=True))
        p.ins("tensor", lambda e: e.matmul(
            psA[:, 512:512 + TS], sb["selB"][:, :], sb["brow16"][:, TS:2 * TS],
            start=True, stop=True))
        p.stage()
        p.ins("vector", lambda e, sl=sl: e.tensor_copy(
            out=BT2[:, 0:2048][:, sl], in_=psA[:, 0:TS]))
        p.ins("vector", lambda e, sl=sl: e.tensor_copy(
            out=BT2[:, 2048:4096][:, sl], in_=psA[:, 512:512 + TS]))
        p.stage()
        for m in range(19):
            if m < 16:
                bank = m % 4
                p.ins("tensor", lambda e, m=m, bank=bank: e.matmul(
                    psA[:, bank * 512:bank * 512 + 512], sb["wdown"][0:64, m, :],
                    sb["wdown"][0:64, 16:20, :].rearrange("p a b -> p (a b)"), start=True, stop=True))
            if 0 <= m - 1 < 16:
                i = m - 1
                bank = i % 4
                p.ins("scalar", lambda e, i=i, bank=bank: e.activation(
                    out=sb["rowarena"][:, 1, (i % 2) * 512:(i % 2) * 512 + 512],
                    in_=psA[:, bank * 512:bank * 512 + 512],
                    func=AF.Exp, bias=sb["l0c"][:, li, i, 5:6]))
            if 0 <= m - 2 < 16:
                i = m - 2
                p.ins("scalar", lambda e, i=i: e.activation(
                    out=DSUB[:, i, :], in_=sb["rowarena"][:, 1, (i % 2) * 512:(i % 2) * 512 + 512],
                    func=AF.Ln, bias=1.0,
                    accum_out=sb["sdac"][:, i % 2:i % 2 + 1]))
            if 0 <= m - 3 < 16:
                i = m - 3
                p.ins("vector", lambda e, i=i: e.tensor_add(
                    out=SDEL[:, i:i + 1], in0=SDEL[:, i:i + 1],
                    in1=sb["sdac"][:, i % 2:i % 2 + 1]))
                p.ins("vector", lambda e, i=i: e.tensor_mul(
                    out=USUB[:, i, :], in0=DSUB[:, i, :], in1=xpre3[:, i, 3:515]))
            p.stage()
        p.dma(out=dr["d_dl"].rearrange("(c p) t -> p c t", p=128)[:, :, sl],
              in_=DSUB[:, :, :])
        p.dma(out=dr["d_uu"].rearrange("(c p) t -> p c t", p=128)[:, :, sl],
              in_=USUB[:, :, :])
        p.stage()

    p.ins("vector", lambda e: e.tensor_copy(out=sb["wtev"][:, 0:48][:, 0:16], in_=SDEL))
    p.stage()
    p.ins("vector", lambda e: e.tensor_sub(out=SDT, in0=SDEL, in1=sb["wtev"][:, 0:48][:, 0:16]))
    p.stage()
    p.ins("vector", lambda e: e.tensor_copy(out=sb["wtev"][:, 0:48][:, 16:32], in_=SDT))
    p.stage()
    p.ins("vector", lambda e: e.tensor_sub(out=SDT, in0=SDT, in1=sb["wtev"][:, 0:48][:, 16:32]))
    p.stage()
    p.ins("vector", lambda e: e.tensor_copy(out=sb["wtev"][:, 0:48][:, 32:48], in_=SDT))
    p.stage()
    for s2 in range(16):
        for j in range(3):
            p.ins("tensor", lambda e, s2=s2, j=j: e.matmul(
                psB[:, s2 * 16:s2 * 16 + 16], sb["selD"][:, s2, :],
                sb["wtev"][:, 0:48][:, j * 16:j * 16 + 16],
                start=(j == 0), stop=(j == 2)))
    p.stage()
    p.ins("vector", lambda e: e.tensor_mul(out=RTMP, in0=ACOL, in1=psB[:, 0:256]))
    p.stage()
    p.ins("scalar", lambda e: e.activation(out=PPROD, in_=RTMP, func=AF.Exp))
    p.stage()

    BT = sb["hb"][:, :, :].rearrange("p a b -> p (a b)")
    bt_t = BT[:, 0:2048]
    ct_t = BT[:, 2048:4096]
    dchunk = [BT[:, 4096:6144], BT[:, 6144:8192]]
    hscb = [BT[:, 8192:8704], BT[:, 8704:9216]]
    yst = [BT[:, 9216:9728], BT[:, 9728:10240]]
    uchunk = [XPB[:, 0:2048], XPB[:, 2048:4096]]
    bxb = [sb["tband"][:, 0:512], sb["tband"][:, 512:1024]]
    dab = [sb["tmpT"][:, 0:512], sb["tmpT"][:, 512:1024]]
    hsb = [sb["rowarena"][:, 0, 0:512], sb["rowarena"][:, 0, 512:1024]]
    DL = dr["d_dl"].rearrange("(c p) t -> p c t", p=128)
    UU = dr["d_uu"].rearrange("(c p) t -> p c t", p=128)
    YY = dr["d_yy"].rearrange("(c p) t -> p c t", p=128)
    RD = [psA[:, 0:512], psA[:, 512:1024]]
    RU = [psA[:, 1024:1536], psA[:, 1536:2048], psC[:, 0:512]]
    ACC = [psB[:, 0:512], psB[:, 512:1024]]
    units = [(c, q, s2) for c in range(16) for q in range(4) for s2 in range(16)]
    NU = len(units)
    p.dma(out=dchunk[0], in_=DL[:, 0, :])
    p.dma(out=uchunk[0], in_=UU[:, 0, :])
    p.stage()
    for j in range(NU + 7):
        if j < NU:
            c, q, s2 = units[j]
            qsl = slice(q * 512, (q + 1) * 512)
            p.ins("tensor", lambda e, j=j, s2=s2, c=c, qsl=qsl: e.matmul(
                RD[j % 2], sb["selD"][:, s2, :], dchunk[c % 2][:, qsl],
                start=True, stop=True))
            p.ins("tensor", lambda e, j=j, s2=s2, c=c, qsl=qsl: e.matmul(
                RU[j % 3], sb["selD"][:, s2, :], uchunk[c % 2][:, qsl],
                start=True, stop=True))
            if q == 0 and s2 == 1 and c + 1 < 16:
                p.dma(out=dchunk[(c + 1) % 2], in_=DL[:, c + 1, :])
                p.dma(out=uchunk[(c + 1) % 2], in_=UU[:, c + 1, :])
        if 0 <= j - 1 < NU:
            i = j - 1
            g = units[i][2] * 16 + units[i][0]
            p.ins("scalar", lambda e, i=i, g=g: e.activation(
                out=dab[i % 2][:, :], in_=RD[i % 2], func=AF.Exp,
                scale=ACOL[:, g:g + 1]))
        if 0 <= j - 2 < NU:
            i = j - 2
            c, q, s2 = units[i]
            qsl = slice(q * 512, (q + 1) * 512)
            g = s2 * 16 + c
            p.ins("vector", lambda e, i=i, qsl=qsl: e.tensor_tensor(
                out=bxb[i % 2][:, :], in0=RU[i % 3], in1=bt_t[:, qsl], op=ALU.mult))
            p.ins("vector", lambda e, i=i, q=q, s2=s2: e.tensor_tensor_scan(
                out=hsb[i % 2][:, :], data0=dab[i % 2][:, :],
                data1=bxb[i % 2][:, :],
                initial=(0.0 if q == 0 else CARRY[:, s2:s2 + 1]),
                op0=ALU.mult, op1=ALU.add))
        if 0 <= j - 3 < NU:
            i = j - 3
            c, q, s2 = units[i]
            g = s2 * 16 + c
            qsl = slice(q * 512, (q + 1) * 512)
            p.ins("gpsimd", lambda e, i=i, qsl=qsl: e.tensor_tensor(
                out=hscb[i % 2][:, :], in0=hsb[i % 2][:, :], in1=ct_t[:, qsl],
                op=ALU.mult))
            if q == 3:
                p.ins("vector", lambda e, i=i, g=g: e.tensor_copy(
                    out=HLOC[:, g:g + 1], in_=hsb[i % 2][:, 511:512]))
            else:
                p.ins("vector", lambda e, i=i, s2=s2: e.tensor_copy(
                    out=CARRY[:, s2:s2 + 1], in_=hsb[i % 2][:, 511:512]))
        if 0 <= j - 4 < NU:
            i = j - 4
            c, q, s2 = units[i]
            cq = (c * 4 + q) % 2
            p.ins("tensor", lambda e, i=i, s2=s2, cq=cq: e.matmul(
                ACC[cq], sb["selY"][:, s2, :], hscb[i % 2][:, :],
                start=(s2 == 0), stop=(s2 == 15)))
        if 0 <= j - 5 < NU:
            i = j - 5
            c, q, s2 = units[i]
            if s2 == 15:
                cq = (c * 4 + q) % 2
                p.ins("scalar", lambda e, cq=cq: e.activation(
                    out=yst[cq], in_=ACC[cq], func=AF.Copy))
        if 0 <= j - 6 < NU:
            i = j - 6
            c, q, s2 = units[i]
            if s2 == 15:
                cq = (c * 4 + q) % 2
                p.dma(out=YY[:, c, q * 512:(q + 1) * 512], in_=yst[cq])
        p.stage()

    p.ins("vector", lambda e: e.tensor_copy(out=sb["rowarena"][:, 2, 0:256], in_=HLOC))
    p.ins("vector", lambda e: e.tensor_copy(out=sb["rowarena"][:, 2, 256:512], in_=PPROD))
    p.stage()
    p.dma(out=dr["c_inR"], in_=sb["rowarena"][:, 2, 0:512][:, :])
    p.stage()
    p.cc("AllGather", ALU.bypass, [dr["c_inR"].opt()], [dr["c_outR"].opt()])
    p.stage()
    p.ins("vector", lambda e: e.memset(HIN, 0.0))
    p.stage()
    for k in range(NCORE - 1):
        p.dma(out=sb["rowarena"][:, 2, 0:512][:, :], in_=dr["c_outR"][k * 128:(k + 1) * 128, :])
        p.stage()
        if k == 0:
            p.ins("vector", lambda e: e.tensor_copy(out=RTMP, in_=sb["rowarena"][:, 2, 0:256]))
            p.stage()
        else:
            p.ins("vector", lambda e: e.tensor_mul(
                out=RTMP2, in0=RTMP, in1=sb["rowarena"][:, 2, 256:512]))
            p.stage()
            p.ins("vector", lambda e: e.tensor_add(
                out=RTMP, in0=RTMP2, in1=sb["rowarena"][:, 2, 0:256]))
            p.stage()
        p.ins("vector", lambda e, k=k: e.scalar_tensor_tensor(
            out=HIN, in0=RTMP, scalar=sb["cmask"][:, k:k + 1], in1=HIN,
            op0=ALU.mult, op1=ALU.add))
        p.stage()

    CDH = XPB[:, 0:2048]; CDM = XPB[:, 2048:4096]; CDL = XPB[:, 4096:6144]
    XCSC = XPB[:, 6144:8192]
    SZC = BT[:, 0:2048]
    YLC = BT[:, 8192:10240]
    GST = XPB[:, 8192:16384].rearrange("p (c t) -> p c t", t=TS)
    XC = dr["d_xc"].rearrange("(c p) t -> p c t", p=128)
    ZZ = dr["d_zz"].rearrange("(c p) t -> p c t", p=128)
    GG = dr["d_gg"].rearrange("(c p) t -> p c t", p=128)
    for c in range(16):
        p.dma(out=dchunk[0], in_=DL[:, c, :])
        p.dma(out=XCSC, in_=XC[:, c, :])
        p.dma(out=SZC, in_=ZZ[:, c, :])
        p.dma(out=YLC, in_=YY[:, c, :])
        p.stage()
        for q in range(4):
            qsl = slice(q * 512, (q + 1) * 512)
            p.ins("vector", lambda e, q=q, qsl=qsl: e.tensor_tensor_scan(
                out=sb["rowarena"][:, 0, qsl],
                data0=sb["cst"][:, 4:5].broadcast_to([128, 512]),
                data1=dchunk[0][:, qsl],
                initial=(0.0 if q == 0 else sb["rowarena"][:, 0, q * 512 - 1:q * 512]),
                op0=ALU.max, op1=ALU.add))
            p.stage()
        p.ins("vector", lambda e: e.tensor_copy(out=CDH, in_=sb["rowarena"][:, 0, :]))
        p.stage()
        p.ins("vector", lambda e: e.tensor_sub(
            out=sb["tmpT"][:, :], in0=sb["rowarena"][:, 0, :], in1=CDH))
        p.stage()
        p.ins("vector", lambda e: e.tensor_copy(out=CDM, in_=sb["tmpT"][:, :]))
        p.stage()
        p.ins("vector", lambda e: e.tensor_sub(
            out=CDL, in0=sb["tmpT"][:, :], in1=CDM))
        p.stage()
        u2 = [(q, s2) for q in range(4) for s2 in range(16)]
        N2 = len(u2)
        for j in range(N2 + 6):
            if j < N2:
                q, s2 = u2[j]
                qsl = slice(q * 512, (q + 1) * 512)
                for t3, cdt in enumerate((CDH, CDM, CDL)):
                    p.ins("tensor", lambda e, j=j, s2=s2, cdt=cdt, t3=t3, qsl=qsl: e.matmul(
                        RD[j % 2], sb["selD"][:, s2, :], cdt[:, qsl],
                        start=(t3 == 0), stop=(t3 == 2)))
            if 0 <= j - 1 < N2:
                i = j - 1
                q, s2 = u2[i]
                g = s2 * 16 + c
                p.ins("scalar", lambda e, i=i, g=g: e.activation(
                    out=QB[i % 2], in_=RD[i % 2], func=AF.Exp,
                    scale=ACOL[:, g:g + 1]))
            if 0 <= j - 2 < N2:
                i = j - 2
                q, s2 = u2[i]
                g = s2 * 16 + c
                qsl = slice(q * 512, (q + 1) * 512)
                p.ins("vector", lambda e, i=i, qsl=qsl: e.tensor_tensor(
                    out=bxb[i % 2][:, :], in0=QB[i % 2], in1=ct_t[:, qsl], op=ALU.mult))
                p.ins("vector", lambda e, i=i, g=g: e.tensor_scalar_mul(
                    out=sb["tband"][:, 1024 + (i % 2) * 512:1024 + (i % 2) * 512 + 512],
                    in0=bxb[i % 2][:, :], scalar1=HIN[:, g:g + 1]))
            if 0 <= j - 3 < N2:
                i = j - 3
                q, s2 = u2[i]
                cq = q % 2
                p.ins("tensor", lambda e, i=i, s2=s2, cq=cq: e.matmul(
                    ACC[cq], sb["selY"][:, s2, :],
                    sb["tband"][:, 1024 + (i % 2) * 512:1024 + (i % 2) * 512 + 512],
                    start=(s2 == 0), stop=(s2 == 15)))
            if 0 <= j - 4 < N2:
                i = j - 4
                q, s2 = u2[i]
                if s2 == 15:
                    cq = q % 2
                    qsl = slice(q * 512, (q + 1) * 512)
                    p.ins("vector", lambda e, c=c, q=q, qsl=qsl, cq=cq: e.scalar_tensor_tensor(
                        out=sb["tmpT"][:, 1536:2048][:, :], in0=XCSC[:, qsl],
                        scalar=sb["l0c"][:, li, c, 6:7], in1=ACC[cq],
                        op0=ALU.mult, op1=ALU.add))
                    p.ins("vector", lambda e, qsl=qsl: e.tensor_add(
                        out=sb["tmpT"][:, 1536:2048][:, :], in0=sb["tmpT"][:, 1536:2048][:, :], in1=YLC[:, qsl]))
                    p.ins("vector", lambda e, c=c, q=q, qsl=qsl: e.tensor_mul(
                        out=GST[:, (c % 4) * 4 + q, :], in0=sb["tmpT"][:, 1536:2048][:, :],
                        in1=SZC[:, qsl]))
            p.stage()
        if c % 4 == 3:
            p.dma(out=GG[:, c - 3:c + 1, :],
                  in_=GST[:, :, :].rearrange("p (a b) c -> p a (b c)", b=4))
            p.stage()

    WOUT = dr["d_wout"].rearrange("(kc p) mm -> p kc mm", p=128)
    GBUF = XPB[:, 0:8192].rearrange("p (c t) -> p c t", t=TS)
    for s in range(NSUB):
        sl = slice(s * TS, (s + 1) * TS)
        p.dma(out=GBUF, in_=GG[:, :, sl])
        p.stage()
        for dc in range(9):
            if dc < DC:
                buf = dc % 2
                if dc == 0:
                    p.dma(out=sb["wA"][:, 0:16, :], in_=WOUT[:, :, 0:128])
                    p.stage()
                if dc + 1 < DC:
                    p.dma(out=sb["wA"][:, 16 * (1 - buf):16 * (1 - buf) + 16, :],
                          in_=WOUT[:, :, (dc + 1) * 128:(dc + 2) * 128])
                bank = dc % 4
                for k in range(16):
                    p.ins("tensor", lambda e, k=k, dc=dc, buf=buf, bank=bank: e.matmul(
                        psA[:, bank * 512:bank * 512 + 512],
                        sb["wA"][:, 16 * buf + k, :], GBUF[:, k, :],
                        start=(k == 0), stop=(k == 15)))
            if dc >= 1:
                i = dc - 1
                bank = i % 4
                p.ins("vector", lambda e, i=i, sl=sl, bank=bank: e.tensor_add(
                    out=sb["x"][:, i, sl], in0=sb["x"][:, i, sl],
                    in1=psA[:, bank * 512:bank * 512 + 512]))
            p.stage()




def emit_amax2_allreduce(p, sb, dr, n):
    p.dma(out=dr["c_in1"][0:1, 0:n], in_=sb["rowS"][0:1, 8:8 + n])
    p.stage()
    p.cc("AllReduce", ALU.max, [dr["c_in1"].opt()], [dr["c_out1"].opt()])
    p.stage()
    p.dma(out=sb["rowS"][0:1, 8:8 + n], in_=dr["c_out1"][0:1, 0:n])
    p.stage()


def emit_colmax_to(p, sb, dst_ap):
    psC = sb["psC"]
    p.ins("tensor", lambda e: e.transpose(
        psC[0:1, 0:128], sb["colA"][:, :], sb["identf"][:, :]))
    p.stage()
    p.ins("vector", lambda e: e.reduce_max(
        out=dst_ap, in_=psC[0:1, 0:128], axis=AX.X))
    p.stage()


def emit_ffn1(p, sb, dr, first_call):
    nc = p.nc
    QM = 32767.0
    psA, psB, psC = sb["psA"], sb["psB"], sb["psC"]
    x_of_c = lambda c: sb["x"][:, c, :]
    if first_call:
        emit_weight_prep(p, sb, dr["f1g"], I1, D, 7.0, dr["wc1g"], blk=32)
        emit_weight_prep(p, sb, dr["f1u"], I1, D, 7.0, dr["wc1u"], blk=32)
        emit_weight_prep(p, sb, dr["f1d"], D, I1, 7.0, dr["wc1d"], blk=32)
    emit_norm(p, sb, x_of_c, None, with_mean=True, amax_only=True)
    emit_colmax_to(p, sb, sb["rowS"][0:1, 8:9])
    emit_amax2_allreduce(p, sb, dr, 1)
    p.ins("vector", lambda e: e.tensor_scalar(
        out=sb["rowS"][0:1, 0:1], in0=sb["rowS"][0:1, 8:9], scalar1=1.0 / QM,
        scalar2=1e-12, op0=ALU.mult, op1=ALU.max))
    p.stage()
    p.ins("vector", lambda e: e.reciprocal(
        out=sb["rowS"][0:1, 1:2], in_=sb["rowS"][0:1, 0:1]))
    p.stage()
    emit_bcast_scale_cols(p, sb, 2)
    p.ins("vector", lambda e: e.tensor_scalar_mul(
        out=sb["rowarena"][0:1, 1, :], in0=sb["rowarena"][0:1, 2, :],
        scalar1=sb["rowS"][0:1, 1:2]))
    p.stage()
    p.ins("vector", lambda e: e.tensor_mul(
        out=sb["rowarena"][0:1, 0, :], in0=sb["rowarena"][0:1, 1, :],
        in1=sb["rowarena"][0:1, 0, :]))
    p.stage()
    p.ins("vector", lambda e: e.tensor_scalar_mul(
        out=sb["rowarena"][0:1, 0, :], in0=sb["rowarena"][0:1, 0, :], scalar1=-1.0))
    p.stage()
    for s in range(NSUB):
        sl = slice(s * TS, (s + 1) * TS)
        p.ins("tensor", lambda e, sl=sl: e.matmul(
            psB[:, 0:TS], sb["ones1_f"][:, :], sb["rowarena"][0:1, 1, sl],
            start=True, stop=True))
        p.ins("tensor", lambda e, sl=sl: e.matmul(
            psC[:, 0:TS], sb["ones1_f"][:, :], sb["rowarena"][0:1, 0, sl],
            start=True, stop=True))
        p.stage()
        for c in range(DC):
            tq = sb["tmpT"][:, 0:TS]
            p.ins("vector", lambda e, c=c, sl=sl, tq=tq: e.tensor_mul(
                out=tq, in0=x_of_c(c)[:, sl], in1=psB[:, 0:TS]))
            p.ins("vector", lambda e, tq=tq: e.tensor_add(
                out=tq, in0=tq, in1=psC[:, 0:TS]))
            p.ins("vector", lambda e, tq=tq: e.tensor_scalar(
                out=tq, in0=tq, scalar1=MAGIC, scalar2=MAGIC,
                op0=ALU.add, op1=ALU.subtract))
            p.ins("vector", lambda e, c=c, sl=sl, tq=tq: e.tensor_scalar(
                out=sb["hb"][:, c, sl], in0=tq, scalar1=QM, scalar2=sb["colS"][:, 0:1],
                op0=ALU.min, op1=ALU.mult))
        p.stage()
    first_max = True
    for ic in range(I1C):
        for k in range(DC):
            p.dma(out=sb["wA"][:, k, :],
                  in_=dr["wc1g"][k * 128:(k + 1) * 128, ic * 128:(ic + 1) * 128])
            p.dma(out=sb["wA"][:, DC + k, :],
                  in_=dr["wc1u"][k * 128:(k + 1) * 128, ic * 128:(ic + 1) * 128])
        p.stage()
        for half in range(2):
            hsl = slice(half * (T // 2), (half + 1) * (T // 2))
            for s2 in range(2):
                sl = slice(half * (T // 2) + s2 * TS, half * (T // 2) + (s2 + 1) * TS)
                po = slice(s2 * TS, (s2 + 1) * TS)
                pu = slice(1024 + s2 * TS, 1024 + (s2 + 1) * TS)
                for k in range(DC):
                    p.ins("tensor", lambda e, k=k, sl=sl, po=po: e.matmul(
                        psA[:, po], sb["wA"][:, k, :], sb["hb"][:, k, sl],
                        start=(k == 0), stop=(k == DC - 1)))
                for k in range(DC):
                    p.ins("tensor", lambda e, k=k, sl=sl, pu=pu: e.matmul(
                        psA[:, pu], sb["wA"][:, DC + k, :], sb["hb"][:, k, sl],
                        start=(k == 0), stop=(k == DC - 1)))
            p.stage()
            p.ins("vector", lambda e: e.tensor_copy(
                out=sb["tband"][:, 0:1024], in_=psA[:, 0:1024]))
            p.ins("scalar", lambda e: e.activation(
                out=sb["tband"][:, 1024:2048], in_=psA[:, 1024:2048], func=AF.Copy))
            p.stage()
            p.dma(out=dr["d_gt"][ic * 128:(ic + 1) * 128, hsl], in_=sb["tband"][:, 0:1024])
            p.dma(out=dr["d_up"][ic * 128:(ic + 1) * 128, hsl], in_=sb["tband"][:, 1024:2048])
            for off, colx in ((0, "colB"), (1024, "colC")):
                if first_max and off == 0:
                    p.ins("vector", lambda e: e.reduce_max(
                        out=sb["colA"][:, :], in_=psA[:, 0:1024], axis=AX.X,
                        apply_absolute_value=True))
                else:
                    p.ins("vector", lambda e, off=off: e.tensor_reduce(
                        out=sb["colB"][:, :], in_=psA[:, off:off + 1024], axis=AX.X,
                        op=ALU.max, apply_absolute_value=True))
                    if off == 0:
                        p.ins("vector", lambda e: e.tensor_max(
                            out=sb["colA"][:, :], in0=sb["colA"][:, :], in1=sb["colB"][:, :]))
                    else:
                        if first_max:
                            p.ins("vector", lambda e: e.tensor_copy(
                                out=sb["colC"][:, :], in_=sb["colB"][:, :]))
                        else:
                            p.ins("vector", lambda e: e.tensor_max(
                                out=sb["colC"][:, :], in0=sb["colC"][:, :], in1=sb["colB"][:, :]))
            if first_max:
                first_max = False
            p.stage()
    emit_colmax_to(p, sb, sb["rowS"][0:1, 8:9])
    p.ins("vector", lambda e: e.tensor_copy(out=sb["colA"][:, :], in_=sb["colC"][:, :]))
    p.stage()
    emit_colmax_to(p, sb, sb["rowS"][0:1, 9:10])
    emit_amax2_allreduce(p, sb, dr, 2)
    for i_, (src_, so, ro) in enumerate((("g", 8, 2), ("u", 9, 4))):
        p.ins("vector", lambda e, so=so, ro=ro: e.tensor_scalar(
            out=sb["rowS"][0:1, ro:ro + 1], in0=sb["rowS"][0:1, so:so + 1],
            scalar1=1.0 / QM, scalar2=1e-12, op0=ALU.mult, op1=ALU.max))
        p.stage()
        p.ins("vector", lambda e, ro=ro: e.reciprocal(
            out=sb["rowS"][0:1, ro + 1:ro + 2], in_=sb["rowS"][0:1, ro:ro + 1]))
        p.stage()
    emit_bcast_scale_cols(p, sb, 8)
    first_max = True
    for ic in range(I1C):
        p.dma(out=sb["tband"][:, 0:2048], in_=dr["d_gt"][ic * 128:(ic + 1) * 128, :])
        p.stage()
        gq = sb["tmpT"][:, :]
        p.ins("vector", lambda e, gq=gq: e.tensor_scalar(
            out=gq, in0=sb["tband"][:, 0:2048], scalar1=sb["colS"][:, 3:4],
            scalar2=MAGIC, op0=ALU.mult, op1=ALU.add))
        p.ins("vector", lambda e, gq=gq: e.tensor_scalar(
            out=gq, in0=gq, scalar1=MAGIC, scalar2=sb["colS"][:, 2:3],
            op0=ALU.subtract, op1=ALU.mult))
        p.stage()
        sg = sb["rowarena"][:, 0, :]
        p.ins("scalar", lambda e, gq=gq, sg=sg: e.activation(
            out=sg, in_=gq, func=AF.Sigmoid))
        p.stage()
        p.ins("vector", lambda e, sg=sg: e.tensor_scalar(
            out=sg, in0=sg, scalar1=65536.0, scalar2=MAGIC,
            op0=ALU.mult, op1=ALU.add))
        p.ins("vector", lambda e, sg=sg: e.tensor_scalar(
            out=sg, in0=sg, scalar1=MAGIC, scalar2=65535.0,
            op0=ALU.subtract, op1=ALU.min))
        p.ins("vector", lambda e, sg=sg: e.tensor_scalar_mul(
            out=sg, in0=sg, scalar1=2.0 ** -16))
        p.stage()
        p.ins("vector", lambda e, gq=gq, sg=sg: e.tensor_tensor(
            out=sb["tband"][:, 0:2048], in0=gq, in1=sg, op=ALU.mult))
        p.stage()
        p.dma(out=dr["d_g2"][ic * 128:(ic + 1) * 128, :], in_=sb["tband"][:, 0:2048])
        if first_max:
            p.ins("vector", lambda e: e.reduce_max(
                out=sb["colA"][:, :], in_=sb["tband"][:, 0:2048], axis=AX.X,
                apply_absolute_value=True))
            first_max = False
        else:
            p.ins("vector", lambda e: e.tensor_reduce(
                out=sb["colB"][:, :], in_=sb["tband"][:, 0:2048], axis=AX.X,
                op=ALU.max, apply_absolute_value=True))
            p.ins("vector", lambda e: e.tensor_max(
                out=sb["colA"][:, :], in0=sb["colA"][:, :], in1=sb["colB"][:, :]))
        p.stage()
    emit_colmax_to(p, sb, sb["rowS"][0:1, 8:9])
    emit_amax2_allreduce(p, sb, dr, 1)
    p.ins("vector", lambda e: e.tensor_scalar(
        out=sb["rowS"][0:1, 6:7], in0=sb["rowS"][0:1, 8:9],
        scalar1=1.0 / QM, scalar2=1e-12, op0=ALU.mult, op1=ALU.max))
    p.stage()
    p.ins("vector", lambda e: e.reciprocal(
        out=sb["rowS"][0:1, 7:8], in_=sb["rowS"][0:1, 6:7]))
    p.stage()
    emit_bcast_scale_cols(p, sb, 8)
    first_max = True
    for ic in range(I1C):
        p.dma(out=sb["tband"][:, 0:2048], in_=dr["d_g2"][ic * 128:(ic + 1) * 128, :])
        p.dma(out=sb["hq_f"][:, 0:4, :].rearrange("p a b -> p (a b)"),
              in_=dr["d_up"][ic * 128:(ic + 1) * 128, :])
        p.stage()
        g2q = sb["tmpT"][:, :]
        p.ins("vector", lambda e, g2q=g2q: e.tensor_scalar(
            out=g2q, in0=sb["tband"][:, 0:2048], scalar1=sb["colS"][:, 7:8],
            scalar2=MAGIC, op0=ALU.mult, op1=ALU.add))
        p.ins("vector", lambda e, g2q=g2q: e.tensor_scalar(
            out=g2q, in0=g2q, scalar1=MAGIC, scalar2=sb["colS"][:, 6:7],
            op0=ALU.subtract, op1=ALU.mult))
        p.stage()
        uq = sb["rowarena"][:, 0, :]
        p.ins("vector", lambda e, uq=uq: e.tensor_scalar(
            out=uq, in0=sb["hq_f"][:, 0:4, :].rearrange("p a b -> p (a b)"),
            scalar1=sb["colS"][:, 5:6], scalar2=MAGIC, op0=ALU.mult, op1=ALU.add))
        p.ins("vector", lambda e, uq=uq: e.tensor_scalar(
            out=uq, in0=uq, scalar1=MAGIC, scalar2=sb["colS"][:, 4:5],
            op0=ALU.subtract, op1=ALU.mult))
        p.stage()
        p.ins("vector", lambda e, g2q=g2q, uq=uq: e.tensor_tensor(
            out=sb["tband"][:, 0:2048], in0=g2q, in1=uq, op=ALU.mult))
        p.stage()
        p.dma(out=dr["big0"][ic * 128:(ic + 1) * 128, :], in_=sb["tband"][:, 0:2048])
        if first_max:
            p.ins("vector", lambda e: e.reduce_max(
                out=sb["colA"][:, :], in_=sb["tband"][:, 0:2048], axis=AX.X,
                apply_absolute_value=True))
            first_max = False
        else:
            p.ins("vector", lambda e: e.tensor_reduce(
                out=sb["colB"][:, :], in_=sb["tband"][:, 0:2048], axis=AX.X,
                op=ALU.max, apply_absolute_value=True))
            p.ins("vector", lambda e: e.tensor_max(
                out=sb["colA"][:, :], in0=sb["colA"][:, :], in1=sb["colB"][:, :]))
        p.stage()
    emit_colmax_to(p, sb, sb["rowS"][0:1, 8:9])
    emit_amax2_allreduce(p, sb, dr, 1)
    p.ins("vector", lambda e: e.tensor_scalar(
        out=sb["rowS"][0:1, 6:7], in0=sb["rowS"][0:1, 8:9],
        scalar1=1.0 / QM, scalar2=1e-12, op0=ALU.mult, op1=ALU.max))
    p.stage()
    p.ins("vector", lambda e: e.reciprocal(
        out=sb["rowS"][0:1, 7:8], in_=sb["rowS"][0:1, 6:7]))
    p.stage()
    emit_bcast_scale_cols(p, sb, 8)
    for s in range(NSUB):
        sl = slice(s * TS, (s + 1) * TS)
        p.dma(out=sb["hq_f"][:, 0:I1C, :],
              in_=dr["big0"].rearrange("(c p) t -> p c t", p=128)[:, 0:I1C, sl])
        p.stage()
        for ic in range(I1C):
            tq = sb["tmpT"][:, 0:TS]
            p.ins("vector", lambda e, ic=ic, tq=tq: e.tensor_scalar(
                out=tq, in0=sb["hq_f"][:, ic, :], scalar1=sb["colS"][:, 7:8],
                scalar2=MAGIC, op0=ALU.mult, op1=ALU.add))
            p.ins("vector", lambda e, ic=ic, tq=tq: e.tensor_scalar(
                out=sb["hq_f"][:, ic, :], in0=tq, scalar1=MAGIC,
                scalar2=sb["colS"][:, 6:7], op0=ALU.subtract, op1=ALU.mult))
        p.stage()
        for c in range(DC):
            p.dma(out=sb["wdown"][:, 0:I1C, :],
                  in_=dr["wc1d"].rearrange("(ic p) m -> p ic m", p=128)[:, 0:I1C, c * 128:(c + 1) * 128])
            p.stage()
            for ic in range(I1C):
                p.ins("tensor", lambda e, c=c, ic=ic: e.matmul(
                    psC[:, 0:TS], sb["wdown"][:, ic, :],
                    sb["hq_f"][:, ic, :], start=(ic == 0), stop=(ic == I1C - 1)))
            p.stage()
            p.ins("vector", lambda e, c=c, sl=sl: e.scalar_tensor_tensor(
                out=sb["x"][:, c, sl], in0=psC[:, 0:TS], scalar=0.5,
                in1=sb["x"][:, c, sl], op0=ALU.mult, op1=ALU.add))
            p.stage()


def emit_attn(p, sb, dr):
    nc = p.nc
    psA, psB, psC = sb["psA"], sb["psB"], sb["psC"]
    x_of_c = lambda c: sb["x"][:, c, :]
    for wi, wn in enumerate(("aWq", "aWk", "aWv")):
        for rc in range(DC):
            p.dma(out=sb["tmpT"][:, 0:1024], in_=dr[wn][rc * 128:(rc + 1) * 128, :])
            p.stage()
            p.ins("gpsimd", lambda e: e.tensor_copy(
                out=sb["tband"][:, 0:1024], in_=sb["tmpT"][:, 0:1024]))
            p.stage()
            p.dma(out=dr["d_win"][rc * 128:(rc + 1) * 128, wi * 1024:(wi + 1) * 1024],
                  in_=sb["tband"][:, 0:1024])
            p.stage()
    p.dma(out=sb["wprep"][0:1, 0:1024], in_=dr["bvrow_h"])
    p.stage()
    HBF = sb["hb"][:, :, :].rearrange("p a b -> p (a b)")
    XPB = sb["hq_f"][:, :, :].rearrange("p a b -> p (a b)")
    hsub = HBF[:, 0:4096].rearrange("p (c t) -> p c t", t=TS)
    qb64 = HBF[0:64, 4096:12288].rearrange("p (c t) -> p c t", t=TS)
    eb64 = HBF[0:64, 12288:13312].rearrange("p (c t) -> p c t", t=64)
    kb64 = XPB[0:64, 0:8192].rearrange("p (c t) -> p c t", t=TS)
    vb64 = XPB[0:64, 8192:16384].rearrange("p (c t) -> p c t", t=1024)
    for s in range(NSUB):
        sl = slice(s * TS, (s + 1) * TS)
        lnq = XPB[:, 0:4096].rearrange("p (c t) -> p c t", t=TS)
        for c in range(DC):
            p.ins("vector", lambda e, c=c, sl=sl: e.tensor_copy(
                out=hsub[:, c, :], in_=x_of_c(c)[:, sl]))
            p.ins("scalar", lambda e, c=c, sl=sl: e.activation(
                out=lnq[:, c, :], in_=x_of_c(c)[:, sl], func=AF.Square))
        p.stage()
        for c in range(DC):
            p.ins("tensor", lambda e, c=c: e.matmul(
                psB[0:1, 0:TS], sb["ones128_bf"][:, :], hsub[:, c, :],
                start=(c == 0), stop=(c == DC - 1)))
        for c in range(DC):
            p.ins("tensor", lambda e, c=c: e.matmul(
                psB[0:1, TS:2 * TS], sb["ones128_bf"][:, :], lnq[:, c, :],
                start=(c == 0), stop=(c == DC - 1)))
        p.stage()
        p.ins("vector", lambda e, sl=sl: e.tensor_scalar_mul(
            out=sb["rowarena"][0:1, 0, sl], in0=psB[0:1, 0:TS], scalar1=1.0 / D))
        p.ins("vector", lambda e, sl=sl: e.tensor_scalar_mul(
            out=sb["rowarena"][0:1, 1, sl], in0=psB[0:1, TS:2 * TS], scalar1=1.0 / D))
        p.stage()
    p.ins("vector", lambda e: e.tensor_mul(
        out=sb["rowarena"][0:1, 2, :], in0=sb["rowarena"][0:1, 0, :],
        in1=sb["rowarena"][0:1, 0, :]))
    p.stage()
    p.ins("vector", lambda e: e.tensor_sub(
        out=sb["rowarena"][0:1, 1, :], in0=sb["rowarena"][0:1, 1, :],
        in1=sb["rowarena"][0:1, 2, :]))
    p.stage()
    p.ins("scalar", lambda e: e.activation(
        out=sb["rowarena"][0:1, 1, :], in_=sb["rowarena"][0:1, 1, :],
        func=AF.Ln, bias=EPS))
    p.stage()
    p.ins("scalar", lambda e: e.activation(
        out=sb["rowarena"][0:1, 2, :], in_=sb["rowarena"][0:1, 1, :],
        func=AF.Exp, scale=-0.5))
    p.stage()
    WQKV = dr["d_win"].rearrange("(kc p) mm -> p kc mm", p=128)
    for s in range(NSUB):
        sl = slice(s * TS, (s + 1) * TS)
        p.ins("tensor", lambda e, sl=sl: e.matmul(
            psC[:, 0:TS], sb["ones1_f"][:, :], sb["rowarena"][0:1, 0, sl],
            start=True, stop=True))
        p.ins("tensor", lambda e, sl=sl: e.matmul(
            psC[:, 512:512 + TS], sb["ones1_f"][:, :], sb["rowarena"][0:1, 2, sl],
            start=True, stop=True))
        p.stage()
        for c in range(DC):
            p.ins("vector", lambda e, c=c, sl=sl: e.tensor_sub(
                out=sb["tmpT"][:, 0:TS], in0=x_of_c(c)[:, sl], in1=psC[:, 0:TS]))
            p.ins("vector", lambda e, c=c: e.tensor_mul(
                out=hsub[:, c, :], in0=sb["tmpT"][:, 0:TS], in1=psC[:, 512:512 + TS]))
        p.stage()
        wvr = XPB[:, 0:8192].rearrange("p (kc mm) -> p kc mm", mm=1024)
        p.dma(out=wvr, in_=WQKV[:, :, 2048:3072])
        p.stage()
        for b in range(8):
            tloc = slice(b * 64, (b + 1) * 64)
            bank = (b % 2) * 2
            for dh in range(2):
                for k in range(DC):
                    p.ins("tensor", lambda e, k=k, dh=dh, tloc=tloc, bank=bank: e.matmul(
                        psA[0:64, (bank + dh) * 512:(bank + dh) * 512 + 512],
                        hsub[:, k, tloc],
                        wvr[:, k, dh * 512:(dh + 1) * 512],
                        start=(k == 0), stop=False))
                p.ins("tensor", lambda e, dh=dh, bank=bank: e.matmul(
                    psA[0:64, (bank + dh) * 512:(bank + dh) * 512 + 512],
                    sb["ones1_f"][0:1, 0:64],
                    sb["wprep"][0:1, dh * 512:(dh + 1) * 512],
                    start=False, stop=True))
            p.stage()
            if b % 2 == 1:
                p.ins("vector", lambda e, b=b: e.tensor_copy(
                    out=vb64[:, b - 1, :], in_=psA[0:64, 0:1024]))
                p.ins("vector", lambda e, b=b: e.tensor_copy(
                    out=vb64[:, b, :], in_=psA[0:64, 1024:2048]))
                p.stage()
        for wi, dst in ((0, qb64), (1, kb64)):
            for dc in range(DC):
                p.dma(out=sb["wA"][:, 16:24, :],
                      in_=WQKV[:, :, wi * 1024 + dc * 128:wi * 1024 + (dc + 1) * 128])
                p.stage()
                for hh in range(2):
                    hd = 2 * dc + hh
                    bank = hd % 4
                    for k in range(DC):
                        p.ins("tensor", lambda e, k=k, hh=hh, bank=bank: e.matmul(
                            psA[0:64, bank * 512:bank * 512 + 512],
                            sb["wA"][:, 16 + k, 64 * hh:64 * hh + 64],
                            hsub[:, k, :],
                            start=(k == 0), stop=(k == DC - 1)))
                    p.stage()
                    p.ins("vector", lambda e, hd=hd, bank=bank, wi=wi, dst=dst: e.tensor_scalar_add(
                        out=dst[:, hd, :], in0=psA[0:64, bank * 512:bank * 512 + 512],
                        scalar1=sb["qkvb"][0:64, wi, hd:hd + 1]))
                    p.stage()
        for b in range(8):
            tloc = slice(b * 64, (b + 1) * 64)
            for hd in range(NHEAD):
                p.ins("tensor", lambda e, hd=hd, tloc=tloc: e.matmul(
                    psB[0:64, hd * 64:hd * 64 + 64],
                    qb64[:, hd, tloc], kb64[:, hd, tloc],
                    start=(hd == 0 or hd == 8), stop=True))
            p.stage()
            p.ins("scalar", lambda e: e.activation(
                out=sb["tmpT"][0:64, 0:1024], in_=psB[0:64, 0:1024], func=AF.Exp,
                scale=0.125))
            p.stage()
            p.ins("vector", lambda e: e.tensor_reduce(
                out=sb["colS"][0:64, 0:16],
                in_=sb["tmpT"][0:64, 0:1024].rearrange("p (g w) -> p g w", w=64),
                axis=AX.X, op=ALU.add))
            p.stage()
            p.ins("vector", lambda e: e.reciprocal(
                out=sb["colS"][0:64, 0:16], in_=sb["colS"][0:64, 0:16]))
            p.stage()
            for h2 in range(0, 16, 8):
                for hh in range(8):
                    hd = h2 + hh
                    p.ins("tensor", lambda e, hd=hd, hh=hh: e.transpose(
                        psC[0:64, hh * 64:hh * 64 + 64],
                        sb["tmpT"][0:64, hd * 64:hd * 64 + 64],
                        sb["identf"][0:64, 0:64]))
                p.stage()
                p.ins("vector", lambda e, h2=h2: e.tensor_copy(
                    out=eb64[:, h2:h2 + 8, :].rearrange("p a b -> p (a b)"),
                    in_=psC[0:64, 0:512]))
                p.stage()
            for hd in range(NHEAD):
                p.ins("tensor", lambda e, hd=hd, b=b: e.matmul(
                    psA[0:64, hd * 64:hd * 64 + 64],
                    eb64[:, hd, :], vb64[:, b, hd * 64:hd * 64 + 64],
                    start=(hd == 0 or hd == 8), stop=True))
            p.stage()
            p.ins("vector", lambda e: e.tensor_tensor(
                out=sb["tmpT"][0:64, 0:1024].rearrange("p (g w) -> p g w", w=64),
                in0=psA[0:64, 0:1024].rearrange("p (g w) -> p g w", w=64),
                in1=sb["colS"][0:64, 0:16].rearrange("p g -> p g ()").broadcast_to([64, 16, 64]),
                op=ALU.mult))
            p.stage()
            for hh in range(8):
                p.ins("tensor", lambda e, hh=hh: e.transpose(
                    psC[:, hh * 64:hh * 64 + 64],
                    sb["tmpT"][0:64, hh * 128:(hh + 1) * 128],
                    sb["identf"][0:64, 0:64]))
            p.stage()
            for dc in range(DC):
                tcol = slice(s * TS + b * 64, s * TS + (b + 1) * 64)
                p.ins("vector", lambda e, dc=dc, tcol=tcol: e.tensor_add(
                    out=sb["x"][:, dc, tcol], in0=sb["x"][:, dc, tcol],
                    in1=psC[:, dc * 64:dc * 64 + 64]))
            p.stage()

def make_dram(nc):
    dr = {}
    def inp(name, shape):
        dr[name] = nc.dram_tensor(name, shape, F32, kind="ExternalInput").ap()
    inp("xT", [D, T])
    inp("f0g", [I0, D]); inp("f0u", [I0, D]); inp("f0d", [D, I0])
    inp("clip2", [1, 2])
    dr["identb_h"] = nc.dram_tensor("identb_h", [128, 128], BF16, kind="ExternalInput").ap()
    inp("identf_h", [128, 128])
    dr["out"] = nc.dram_tensor("out", [D, T], F32, kind="ExternalOutput").ap()
    dr["wc0g"] = nc.dram_tensor("wc0g", [D, I0], BF16, kind="Internal").ap()
    dr["wc0u"] = nc.dram_tensor("wc0u", [D, I0], BF16, kind="Internal").ap()
    dr["wc0d"] = nc.dram_tensor("wc0d", [I0, D], BF16, kind="Internal").ap()
    dr["big0"] = nc.dram_tensor("big0", [I1, T], BF16, kind="Internal").ap()
    dr["c_in1"] = nc.dram_tensor("c_in1", [1, 16], F32, kind="Internal").ap()
    dr["c_out1"] = nc.dram_tensor("c_out1", [1, 16], F32, kind="Internal").ap()
    dr["d_x"] = nc.dram_tensor("d_x", [D, T], F32, kind="Internal").ap()
    for nm in ("d_pre", "d_dl", "d_uu", "d_xc", "d_zz", "d_yy", "d_gg"):
        dr[nm] = nc.dram_tensor(nm, [DI, T], BF16, kind="Internal").ap()
    dr["d_win"] = nc.dram_tensor("d_win", [D, 2 * DI], BF16, kind="Internal").ap()
    dr["d_wout"] = nc.dram_tensor("d_wout", [DI, D], BF16, kind="Internal").ap()
    dr["c_inh"] = nc.dram_tensor("c_inh", [128, 48], F32, kind="Internal").ap()
    dr["c_outh"] = nc.dram_tensor("c_outh", [1024, 48], F32, kind="Internal").ap()
    dr["c_inR"] = nc.dram_tensor("c_inR", [128, 512], F32, kind="Internal").ap()
    dr["c_outR"] = nc.dram_tensor("c_outR", [1024, 512], F32, kind="Internal").ap()
    dr["l0inT"] = nc.dram_tensor("l0inT", [2, D, 2 * DI], F32, kind="ExternalInput").ap()
    dr["l0outT"] = nc.dram_tensor("l0outT", [2, DI, D], F32, kind="ExternalInput").ap()
    dr["l0xpT"] = nc.dram_tensor("l0xpT", [2, DI, 96], F32, kind="ExternalInput").ap()
    dr["l0dtT"] = nc.dram_tensor("l0dtT", [2, 64, DI], F32, kind="ExternalInput").ap()
    dr["l0alog"] = nc.dram_tensor("l0alog", [128, 512], F32, kind="ExternalInput").ap()
    dr["l0cols"] = nc.dram_tensor("l0cols", [128, 2, 16, 8], F32, kind="ExternalInput").ap()
    dr["nw_h"] = nc.dram_tensor("nw_h", [128, 2, DC], F32, kind="ExternalInput").ap()
    dr["cmask_h"] = nc.dram_tensor("cmask_h", [128, 8], F32, kind="ExternalInput").ap()
    dr["selD_h"] = nc.dram_tensor("selD_h", [128, 16 * 128], BF16, kind="ExternalInput").ap()
    dr["selY_h"] = nc.dram_tensor("selY_h", [128, 16 * 128], BF16, kind="ExternalInput").ap()
    dr["selB_h"] = nc.dram_tensor("selB_h", [16, 128], BF16, kind="ExternalInput").ap()
    dr["f1g"] = nc.dram_tensor("f1g", [I1, D], F32, kind="ExternalInput").ap()
    dr["f1u"] = nc.dram_tensor("f1u", [I1, D], F32, kind="ExternalInput").ap()
    dr["f1d"] = nc.dram_tensor("f1d", [D, I1], F32, kind="ExternalInput").ap()
    dr["aWq"] = nc.dram_tensor("aWq", [D, D], F32, kind="ExternalInput").ap()
    dr["aWk"] = nc.dram_tensor("aWk", [D, D], F32, kind="ExternalInput").ap()
    dr["aWv"] = nc.dram_tensor("aWv", [D, D], F32, kind="ExternalInput").ap()
    dr["qkvb_h"] = nc.dram_tensor("qkvb_h", [128, 2, 16], F32, kind="ExternalInput").ap()
    dr["bvrow_h"] = nc.dram_tensor("bvrow_h", [1, D], F32, kind="ExternalInput").ap()
    dr["wc1g"] = nc.dram_tensor("wc1g", [D, I1], BF16, kind="Internal").ap()
    dr["wc1u"] = nc.dram_tensor("wc1u", [D, I1], BF16, kind="Internal").ap()
    dr["wc1d"] = nc.dram_tensor("wc1d", [I1, D], BF16, kind="Internal").ap()
    dr["d_gt"] = nc.dram_tensor("d_gt", [I1, T], BF16, kind="Internal").ap()
    dr["d_up"] = nc.dram_tensor("d_up", [I1, T], BF16, kind="Internal").ap()
    dr["d_g2"] = nc.dram_tensor("d_g2", [I1, T], BF16, kind="Internal").ap()
    return dr


def build_program(upto=99):
    nc = bass.Bass("TRN2", target_bir_lowering=False)
    dr = make_dram(nc)
    stack = contextlib.ExitStack()
    sb = {}

    def S(name, shape, dtype):
        sb[name] = stack.enter_context(nc.sbuf_tensor(name, shape, dtype))

    with stack:
        S("x", [128, DC, T], F32)
        S("hb", [128, DC, T], BF16)
        S("tmpT", [128, T], F32)
        S("tband", [128, T], BF16)
        S("wprep", [128, I1], F32)
        S("wtev", [128, 256], BF16)
        S("wA", [128, 32, 128], BF16)
        S("wdown", [128, 34, 128], BF16)
        S("hq_f", [128, 33, TS], BF16)
        S("rowarena", [128, 3, T], F32)
        S("brow16", [16, 2 * TS], BF16)
        S("selD", [128, 16, 128], BF16)
        S("selY", [128, 16, 128], BF16)
        S("selB", [16, 128], BF16)
        S("nwcol", [128, 2, DC], F32)
        S("l0c", [128, 2, 16, 8], F32)
        S("cmask", [128, 8], F32)
        S("sdac", [128, 2], F32)
        S("rowS", [1, 16], F32)
        S("qkvb", [128, 2, 16], F32)
        S("colS", [128, 16], F32)
        S("colA", [128, 1], F32)
        S("colB", [128, 1], F32)
        S("colC", [128, 1], F32)
        S("sc0", [1, 1], F32)
        S("clipx", [1, 1], F32)
        S("clipd", [1, 1], F32)
        S("ones128_bf", [128, 1], BF16)
        S("ones1_f", [1, 128], F32)
        S("cst", [128, 8], F32)
        S("swg", [128, 24], F32)
        S("swu", [128, 24], F32)
        S("swd", [128, 24], F32)
        S("sall", [128, 72], F32)
        S("identb", [128, 128], BF16)
        S("identf", [128, 128], F32)
        sb["psA"] = stack.enter_context(nc.psum_tensor("psA", [128, 2048], F32))
        sb["psB"] = stack.enter_context(nc.psum_tensor("psB", [128, 1024], F32))
        sb["psC"] = stack.enter_context(nc.psum_tensor("psC", [128, 1024], F32))

        sS = stack.enter_context(nc.semaphore("semstage"))
        sD = stack.enter_context(nc.semaphore("semdma"))
        sC = stack.enter_context(nc.semaphore("semcc"))

        p = Prog(nc, sS, sD, sC)
        CONSTVALS = [EPS, -0.5, 0.125, 1.0, 0.0, 65536.0, 2.0 ** -16, -1.0]
        for i, v in enumerate(CONSTVALS):
            nc.const_aps.register(F32, float(v), sb["cst"][:, i:i + 1])
            p.ins("vector", lambda e, i=i, v=v: e.memset(sb["cst"][:, i:i + 1], float(v)))
        p.ins("vector", lambda e: e.memset(sb["ones128_bf"][:, :], 1.0))
        p.ins("vector", lambda e: e.memset(sb["ones1_f"][:, :], 1.0))
        p.dma(out=sb["identb"][:, :], in_=dr["identb_h"])
        p.dma(out=sb["identf"][:, :], in_=dr["identf_h"])
        p.dma(out=sb["x"][:, :, :], in_=dr["xT"].rearrange("(c p) t -> p c t", p=128))
        p.dma(out=sb["nwcol"][:, :, :], in_=dr["nw_h"])
        p.dma(out=sb["cmask"][:, :], in_=dr["cmask_h"])
        p.dma(out=sb["selD"][:, :, :], in_=dr["selD_h"].rearrange("p (a b) -> p a b", b=128))
        p.dma(out=sb["selY"][:, :, :], in_=dr["selY_h"].rearrange("p (a b) -> p a b", b=128))
        p.dma(out=sb["selB"][:, :], in_=dr["selB_h"])
        p.dma(out=sb["qkvb"][:, :, :], in_=dr["qkvb_h"])
        p.dma(out=sb["l0c"][:, :, :, :], in_=dr["l0cols"])
        p.dma(out=sb["clipx"][:, :], in_=dr["clip2"][0:1, 0:1])
        p.dma(out=sb["clipd"][:, :], in_=dr["clip2"][0:1, 1:2])
        p.stage()

        if upto >= 1:
            emit_ffn0(p, sb, dr, first_call=True)
            if upto >= 2:
                emit_mamba(p, sb, dr)
            if upto >= 3:
                emit_ffn0(p, sb, dr, first_call=False)
            if upto >= 4:
                emit_ffn1(p, sb, dr, first_call=True)
            if upto >= 5:
                emit_attn(p, sb, dr)
            if upto >= 6:
                emit_ffn1(p, sb, dr, first_call=False)

        p.dma(out=dr["out"].rearrange("(c p) t -> p c t", p=128),
              in_=sb["x"][:, :, :])
        p.stage()
        p.ins("sync", lambda e: e.nop())
        p.stage()

        with nc.Block() as block:
            p.replay(block)
    return nc, p


def host_prep(inputs):
    ident = np.eye(128)
    x = np.asarray(inputs["L0_attn_x"])[0]
    f0g = np.ascontiguousarray(np.asarray(inputs["L0_ffn_w_gate"], np.float32))
    f0u = np.ascontiguousarray(np.asarray(inputs["L0_ffn_w_up"], np.float32))
    f0d = np.ascontiguousarray(np.asarray(inputs["L0_ffn_w_down"], np.float32))
    clip2 = np.stack([np.asarray(inputs["L0_ffn_clip_x"], np.float32).reshape(()),
                      np.asarray(inputs["L0_ffn_clip_down"], np.float32).reshape(())]).reshape(1, 2)
    identb = np.ascontiguousarray(ident.astype(BF))
    identf = np.ascontiguousarray(ident.astype(np.float32))
    f1g = np.ascontiguousarray(np.asarray(inputs["L1_ffn_w_gate"], np.float32))
    f1u = np.ascontiguousarray(np.asarray(inputs["L1_ffn_w_up"], np.float32))
    f1d = np.ascontiguousarray(np.asarray(inputs["L1_ffn_w_down"], np.float32))
    aWq = np.ascontiguousarray(np.asarray(inputs["L1_attn_Wq"], np.float32))
    aWk = np.ascontiguousarray(np.asarray(inputs["L1_attn_Wk"], np.float32))
    aWv = np.ascontiguousarray(np.asarray(inputs["L1_attn_Wv"], np.float32))
    bq = np.asarray(inputs["L1_attn_bq"], np.float32)
    bk = np.asarray(inputs["L1_attn_bk"], np.float32)
    bv = np.asarray(inputs["L1_attn_bv"], np.float32)
    qkvb = np.zeros((128, 2, 16), np.float32)
    qkvb[0:64, 0, :] = bq.reshape(16, 64).T
    qkvb[0:64, 1, :] = bk.reshape(16, 64).T
    bvrow = np.ascontiguousarray(bv.reshape(1, D))
    inw = np.asarray(inputs["L0_attn_in_proj_w"], np.float32)
    l0inT = np.ascontiguousarray(inw.transpose(0, 2, 1))
    outw = np.asarray(inputs["L0_attn_out_proj_w"], np.float32)
    l0outT = np.ascontiguousarray(outw.transpose(0, 2, 1))
    xpw_ = np.asarray(inputs["L0_attn_x_proj_w"], np.float32)
    l0xpT = np.ascontiguousarray(xpw_.transpose(0, 2, 1))
    dtw_ = np.asarray(inputs["L0_attn_dt_proj_w"], np.float32)
    l0dtT = np.ascontiguousarray(dtw_.transpose(0, 2, 1))
    alog = np.asarray(inputs["L0_attn_A_log"], np.float32)
    pp = np.arange(128)
    gg = np.arange(256)
    ee = 128 * (gg[None, :] % 16) + 8 * (gg[None, :] // 16) + pp[:, None] // 16
    nn_ = np.broadcast_to(pp[:, None] % 16, (128, 256))
    l0alog = np.concatenate([alog[li][ee, nn_] for li in range(2)], axis=1)
    l0alog = np.ascontiguousarray(l0alog)
    cw = np.asarray(inputs["L0_attn_conv_w"], np.float32)
    cb = np.asarray(inputs["L0_attn_conv_b"], np.float32)
    dtb = np.asarray(inputs["L0_attn_dt_proj_b"], np.float32)
    Dp = np.asarray(inputs["L0_attn_D_param"], np.float32)
    cols = np.zeros((128, 2, 16, 8), np.float32)
    for li in range(2):
        cols[:, li, :, 0:4] = cw[li].reshape(16, 128, 4).transpose(1, 0, 2)
        cols[:, li, :, 4] = cb[li].reshape(16, 128).T
        cols[:, li, :, 5] = dtb[li].reshape(16, 128).T
        cols[:, li, :, 6] = Dp[li].reshape(16, 128).T
    nw = np.asarray(inputs["L0_attn_norm_w"], np.float32)
    nw_h = np.ascontiguousarray(nw.reshape(2, DC, 128).transpose(2, 0, 1))
    selD = np.zeros((128, 16, 128), np.float32)
    for s2 in range(16):
        for p_ in range(128):
            selD[8 * s2 + p_ // 16, s2, p_] = 1.0
    selY = np.ascontiguousarray(selD.transpose(2, 1, 0))
    selB = np.zeros((16, 128), np.float32)
    for p_ in range(128):
        selB[p_ % 16, p_] = 1.0
    selD_h = np.ascontiguousarray(selD.reshape(128, 16 * 128).astype(BF))
    selY_h = np.ascontiguousarray(selY.reshape(128, 16 * 128).astype(BF))
    selB_h = np.ascontiguousarray(selB.astype(BF))
    maps = []
    for c in range(NCORE):
        xs = np.ascontiguousarray(np.asarray(x[c * T:(c + 1) * T, :], np.float32).T)
        cmask = np.zeros((128, 8), np.float32)
        if c > 0:
            cmask[:, c - 1] = 1.0
        maps.append(dict(xT=xs, f0g=f0g, f0u=f0u, f0d=f0d, clip2=clip2,
                         identb_h=identb, identf_h=identf,
                         l0inT=l0inT, l0outT=l0outT, l0xpT=l0xpT, l0dtT=l0dtT,
                         l0alog=l0alog, l0cols=cols, nw_h=nw_h, cmask_h=cmask,
                         selD_h=selD_h, selY_h=selY_h, selB_h=selB_h,
                         f1g=f1g, f1u=f1u, f1d=f1d, aWq=aWq, aWk=aWk, aWv=aWv,
                         qkvb_h=qkvb, bvrow_h=bvrow))
    return maps


def kernel(**inputs):
    nc, _ = build_program()
    in_maps = host_prep(inputs)
    res = run_bass_kernel_spmd(nc, in_maps, core_ids=list(range(NCORE)))
    outs = [r["out"] for r in res.results]
    full = np.concatenate([o.T for o in outs], axis=0)
    return np.ascontiguousarray(full[None, :, :])
